# version 12
# speedup vs baseline: 1.7188x; 1.0137x over previous
import contextlib
import math

import numpy as np
import ml_dtypes
import concourse.bass as bass
import concourse.mybir as mybir
from concourse.bass_utils import run_bass_kernel_spmd
from concourse.alu_op_type import AluOpType

F32 = mybir.dt.float32
BF16 = mybir.dt.bfloat16
AF = mybir.ActivationFunctionType
AX = mybir.AxisListType

S = 4096
D = 1024
NC = 8
NH = 16
DH = 64
NT = S // 128
NCC = D // 128

BF = ml_dtypes.bfloat16



def host_constants():
    c = {}
    c["ident"] = np.eye(128, dtype=BF)
    pos = np.arange(S, dtype=np.float32)[:, None]
    div = np.exp(np.arange(0, D, 2, dtype=np.float32)
                 * np.float32(-np.log(10000.0) / D))
    pe = np.zeros((S, D), dtype=np.float32)
    pe[:, 0::2] = np.sin(pos * div)
    pe[:, 1::2] = np.cos(pos * div)
    c["pe"] = pe
    inv_freq = (1.0 / 10000.0 ** (np.arange(0, DH, 2, dtype=np.float32) / DH))
    t = np.arange(S, dtype=np.float32)
    freqs = t[:, None] * inv_freq[None, :]
    rope = np.concatenate((freqs, freqs), axis=-1)
    ang = rope[:, ::2]
    cos, sin = np.cos(ang), np.sin(ang)
    Cfull = np.empty((S, DH), dtype=np.float32)
    Sfull = np.empty((S, DH), dtype=np.float32)
    Cfull[:, 0::2] = cos
    Cfull[:, 1::2] = cos
    Sfull[:, 0::2] = -sin
    Sfull[:, 1::2] = sin
    c["ropeC"] = np.concatenate([Cfull.T, Cfull.T], axis=0).copy()
    c["ropeS"] = np.concatenate([Sfull.T, Sfull.T], axis=0).copy()
    P = np.zeros((128, 128), dtype=BF)
    for i in range(64):
        P[2 * i, 2 * i + 1] = 1
        P[2 * i + 1, 2 * i] = 1
    c["swapP"] = P
    return c



def ensure_const(nc, value, ap):
    reg = getattr(nc, "_const_reg", None)
    if reg is None:
        reg = set()
        nc._const_reg = reg
    if value not in reg:
        nc.const_aps.register(F32, value, ap)
        reg.add(value)
        return True
    return False


class Phase:

    def __init__(self, nc, tag, sem_stack=None):
        self.nc = nc
        self.tag = tag
        self.stack = contextlib.ExitStack()
        self.sem_stack = sem_stack if sem_stack is not None else self.stack
        self.sems = {}

    def sem(self, name):
        h = self.sem_stack.enter_context(
            self.nc.semaphore(f"{self.tag}_{name}"))
        self.sems[name] = h
        return h

    def sbuf(self, name, shape, dtype):
        return self.stack.enter_context(
            self.nc.sbuf_tensor(f"{self.tag}_{name}", shape, dtype))

    def psum(self, name, shape, dtype):
        return self.stack.enter_context(
            self.nc.psum_tensor(f"{self.tag}_{name}", shape, dtype))

    def close(self):
        self.stack.close()


def emit_ln_stats(vector, scalar, src2d, negm_col, rstd_col, sum_col, ssq_col,
                  sqdump, s_red, red_val, s_stats, eps=1e-5, dim=None):
    dim = dim or src2d.shape[-1]
    vector.reduce_sum(sum_col, src2d, axis=AX.X).then_inc(s_red, 1)
    scalar.wait_ge(s_red, red_val)
    scalar.mul(negm_col, sum_col, -1.0 / dim)
    scalar.activation(sqdump, src2d, AF.Square, bias=negm_col,
                      accum_out=ssq_col)
    scalar.activation(rstd_col, ssq_col, AF.Ln, scale=1.0 / dim, bias=eps)
    scalar.activation(rstd_col, rstd_col, AF.Exp, scale=-0.5).then_inc(
        s_stats, 1)



def build_phase3(nc, x3_ap, out_ap, wrT_ap, wkT_ap, wvT_ap, mur_ap, muk_ap,
                 ident_ap, tag="p3", sem_stack=None, gcbuf=None):
    ph = Phase(nc, tag, sem_stack)
    QT = 1024
    NQ = S // QT
    JQ = QT // 128
    TC = QT // 512

    x3b = ph.sbuf("x3b", [128, JQ, D], F32)
    xnT = ph.sbuf("xnT", [128, NCC, QT + 1], BF16)
    dbuf = ph.sbuf("dbuf", [128, NCC, QT], BF16)
    bufA = ph.sbuf("bufA", [128, NCC, QT], BF16)
    rT = ph.sbuf("rT", [128, NCC, QT], BF16)
    kactT = ph.sbuf("kactT", [128, NCC, QT], BF16)
    wr = ph.sbuf("wr", [128, NCC, D], BF16)
    wk = ph.sbuf("wk", [128, NCC, D], BF16)
    wv = ph.sbuf("wv", [128, NCC, D], BF16)
    mur = ph.sbuf("mur", [128, NCC], F32)
    muk = ph.sbuf("muk", [128, NCC], F32)
    ident = ph.sbuf("ident", [128, 128], BF16)
    xnbf = ph.sbuf("xnbf", [128, JQ, D], BF16)
    sums = ph.sbuf("sums", [128, JQ], F32)
    negm = ph.sbuf("negm", [128, JQ], F32)
    ssq = ph.sbuf("ssq", [128, JQ], F32)
    rstd = ph.sbuf("rstd", [128, JQ], F32)
    sqdump = ph.sbuf("sqdump", [128, D], F32)
    tbuf = ph.sbuf("tbuf", [128, QT], BF16)
    krelu = ph.sbuf("krelu", [128, 2, 512], BF16)
    kcol = ph.sbuf("kcol", [128, NCC, 1], BF16)
    outb = ph.sbuf("outb", [128, 2, D], F32)
    cbuf = gcbuf if gcbuf is not None else ph.sbuf("cbuf", [128, 4], F32)
    ensure_const(nc, 1e-5, cbuf[:, 0:1])

    TP = ph.psum("TP", [128, 8, 128], BF16)
    M0 = ph.psum("M0", [128, 512], F32)
    M1 = ph.psum("M1", [128, 512], F32)
    TB = ph.psum("TB", [128, 8, 128], BF16)

    s_x3 = ph.sem("x3")
    s_w = ph.sem("w")
    s_c = ph.sem("c")
    s_xn = ph.sem("xn")
    s_stats = ph.sem("stats")
    s_red = ph.sem("red")
    s_tp = ph.sem("tp")
    s_ev = ph.sem("ev")
    s_keep = ph.sem("keep")
    s_col0 = ph.sem("col0")
    s_mix = ph.sem("mix")
    s_mm = ph.sem("mm")
    s_pev = ph.sem("pev")
    s_ksq = ph.sem("ksq")
    s_tb = ph.sem("tb")
    s_add = ph.sem("add")
    s_st0 = ph.sem("st0")
    s_st1 = ph.sem("st1")
    s_k = ph.sem("k")
    s_act = ph.sem("act")

    with nc.Block() as block:

        @block.sync
        def _(sync):
            sync.dma_start(out=ident[:], in_=ident_ap[:]).then_inc(s_c, 16)
            sync.dma_start(out=mur[:], in_=mur_ap.rearrange(
                "(c p) -> p c", p=128),
                allow_slow_non_contiguous=True).then_inc(s_c, 16)
            sync.dma_start(out=muk[:], in_=muk_ap.rearrange(
                "(c p) -> p c", p=128),
                allow_slow_non_contiguous=True).then_inc(s_c, 16)
            for q in range(NQ):
                if q > 0:
                    sync.wait_ge(s_add, 8 * q)
                for h in range(2):
                    r0 = q * QT + h * 512
                    sync.dma_start(
                        out=x3b[:, h * 4:(h + 1) * 4, :],
                        in_=x3_ap[r0:r0 + 512, :].rearrange(
                            "(j p) f -> p j f", p=128),
                    ).then_inc(s_x3, 16)
                for j in range(JQ):
                    gj = q * JQ + j
                    sync.wait_ge(s_add, gj + 1)
                    t0 = q * QT + j * 128
                    sync.dma_start(
                        out=out_ap[t0:t0 + 128, :], in_=outb[:, j % 2, :],
                    ).then_inc(s_st0 if gj % 2 == 0 else s_st1, 16)
            sync.wait_ge(s_st0, 16 * (NT // 2))
            sync.wait_ge(s_st1, 16 * (NT // 2))

        @block.gpsimd
        def _(gpsimd):
            gpsimd.dma_start(out=wr[:], in_=wrT_ap.rearrange(
                "(c p) f -> p c f", p=128)).then_inc(s_w, 16)
            gpsimd.dma_start(out=wk[:], in_=wkT_ap.rearrange(
                "(c p) f -> p c f", p=128)).then_inc(s_w, 16)
            gpsimd.dma_start(out=wv[:], in_=wvT_ap.rearrange(
                "(c p) f -> p c f", p=128)).then_inc(s_w, 16)
            gpsimd.memset(cbuf[:, 0:1], 1e-5).then_inc(s_k, 1)
            gpsimd.memset(xnT[:, :, 0:1], 0.0).then_inc(s_col0, 1)
            for q in range(NQ - 1):
                gpsimd.wait_ge(s_ev, 8 * (q + 1))
                gpsimd.tensor_copy(kcol[:], xnT[:, :, QT:QT + 1]).then_inc(
                    s_keep, 1)
                gpsimd.wait_ge(s_mix, 2 * (q + 1))
                gpsimd.tensor_copy(xnT[:, :, 0:1], kcol[:]).then_inc(
                    s_col0, 1)

        @block.tensor
        def _(tensor):
            tensor.wait_ge(s_c, 48)
            for q in range(NQ):
                for j in range(JQ):
                    gj = q * JQ + j
                    tensor.wait_ge(s_xn, gj + 1)
                    if gj > 0:
                        tensor.wait_ge(s_ev, gj)
                    for cc in range(NCC):
                        inst = tensor.transpose(
                            TP[:, cc, :],
                            xnbf[:, j, cc * 128:(cc + 1) * 128],
                            ident[:],
                        )
                    inst.then_inc(s_tp, 1)
                wq = [48, 48, 48]
                for seg, (wt, src) in enumerate(
                        [(wr, bufA), (wk, bufA), (wv, kactT)]):
                    tensor.wait_ge(s_w, wq[seg])
                    if seg < 2:
                        tensor.wait_ge(s_mix, 2 * q + seg + 1)
                    else:
                        tensor.wait_ge(s_ksq, 16 * (q + 1))
                    for d2c in range(NCC):
                        for tc in range(TC):
                            g = (q * 3 + seg) * 16 + d2c * TC + tc
                            M = M0 if g % 2 == 0 else M1
                            if g >= 2:
                                tensor.wait_ge(s_pev, g - 1)
                            for cc in range(NCC):
                                inst = tensor.matmul(
                                    M[:],
                                    wt[:, cc, d2c * 128:(d2c + 1) * 128],
                                    src[:, cc, tc * 512:(tc + 1) * 512],
                                    start=(cc == 0), stop=(cc == NCC - 1),
                                )
                            inst.then_inc(s_mm, 1)
                for j in range(JQ):
                    gj = q * JQ + j
                    tensor.wait_ge(s_pev, (q * 3 + 3) * 16)
                    if gj > 0:
                        tensor.wait_ge(s_add, gj)
                    for d2c in range(NCC):
                        inst = tensor.transpose(
                            TB[:, d2c, :],
                            bufA[:, d2c, j * 128:(j + 1) * 128],
                            ident[:],
                        )
                    inst.then_inc(s_tb, 1)

        @block.scalar
        def _(scalar):
            scalar.wait_ge(s_k, 1)
            actn = [0]

            def fence(inst):
                actn[0] += 1
                inst.then_inc(s_act, 1)
                scalar.wait_ge(s_act, actn[0])

            for q in range(NQ):
                for j in range(JQ):
                    gj = q * JQ + j
                    scalar.wait_ge(s_red, gj + 1)
                    fence(scalar.mul(negm[:, j:j + 1], sums[:, j:j + 1],
                                     -1.0 / D))
                    fence(scalar.activation(sqdump[:], x3b[:, j, :], AF.Square,
                                            bias=negm[:, j:j + 1],
                                            accum_out=ssq[:, j:j + 1]))
                    fence(scalar.mul(rstd[:, j:j + 1], ssq[:, j:j + 1],
                                     1.0 / D))
                    fence(scalar.add(rstd[:, j:j + 1], rstd[:, j:j + 1], 1e-5))
                    fence(scalar.activation(rstd[:, j:j + 1], rstd[:, j:j + 1],
                                            AF.Ln))
                    scalar.activation(rstd[:, j:j + 1], rstd[:, j:j + 1],
                                      AF.Exp, scale=-0.5).then_inc(s_stats, 1)
                for seg in range(2):
                    for d2c in range(NCC):
                        for tc in range(TC):
                            g = (q * 3 + seg) * 16 + d2c * TC + tc
                            M = M0 if g % 2 == 0 else M1
                            scalar.wait_ge(s_mm, g + 1)
                            if seg == 0:
                                scalar.activation(
                                    rT[:, d2c, tc * 512:(tc + 1) * 512],
                                    M[:], AF.Sigmoid).then_inc(s_pev, 1)
                            else:
                                kk = d2c * TC + tc
                                if kk >= 2:
                                    scalar.wait_ge(s_ksq, 16 * q + kk - 1)
                                scalar.activation(
                                    krelu[:, kk % 2, :], M[:], AF.Relu,
                                ).then_inc(s_pev, 1)

        @block.vector
        def _(vector):
            for q in range(NQ):
                for j in range(JQ):
                    gj = q * JQ + j
                    vector.wait_ge(s_x3, 32 * (q + 1))
                    vector.reduce_sum(sums[:, j:j + 1], x3b[:, j, :],
                                      axis=AX.X).then_inc(s_red, 1)
                for j in range(JQ):
                    gj = q * JQ + j
                    vector.wait_ge(s_stats, gj + 1)
                    vector.tensor_scalar(
                        xnbf[:, j, :], x3b[:, j, :], negm[:, j:j + 1],
                        rstd[:, j:j + 1], AluOpType.add, AluOpType.mult,
                    ).then_inc(s_xn, 1)
                if q > 0:
                    vector.wait_ge(s_keep, q)
                for j in range(JQ):
                    gj = q * JQ + j
                    vector.wait_ge(s_tp, gj + 1)
                    vector.tensor_copy(
                        xnT[:, :, 1 + j * 128:1 + (j + 1) * 128], TP[:],
                    ).then_inc(s_ev, 1)
                vector.wait_ge(s_c, 48)
                vector.wait_ge(s_col0, q + 1)
                if q > 0:
                    vector.wait_ge(s_tb, 8 * q)
                for cc in range(NCC):
                    vector.tensor_tensor(dbuf[:, cc, :], xnT[:, cc, 1:QT + 1],
                                         xnT[:, cc, 0:QT], AluOpType.subtract)
                    vector.tensor_scalar(tbuf[:], dbuf[:, cc, :],
                                         mur[:, cc:cc + 1], None,
                                         AluOpType.mult)
                    inst = vector.tensor_tensor(bufA[:, cc, :],
                                                xnT[:, cc, 0:QT],
                                                tbuf[:], AluOpType.add)
                inst.then_inc(s_mix, 1)
                vector.wait_ge(s_mm, (q * 3 + 1) * 16)
                for cc in range(NCC):
                    vector.tensor_scalar(tbuf[:], dbuf[:, cc, :],
                                         muk[:, cc:cc + 1], None,
                                         AluOpType.mult)
                    inst = vector.tensor_tensor(bufA[:, cc, :],
                                                xnT[:, cc, 0:QT],
                                                tbuf[:], AluOpType.add)
                inst.then_inc(s_mix, 1)
                for d2c in range(NCC):
                    for tc in range(TC):
                        kk = d2c * TC + tc
                        vector.wait_ge(s_pev, (q * 3 + 1) * 16 + kk + 1)
                        vector.tensor_tensor(
                            kactT[:, d2c, tc * 512:(tc + 1) * 512],
                            krelu[:, kk % 2, :], krelu[:, kk % 2, :],
                            AluOpType.mult).then_inc(s_ksq, 1)
                for d2c in range(NCC):
                    for tc in range(TC):
                        g = (q * 3 + 2) * 16 + d2c * TC + tc
                        M = M0 if g % 2 == 0 else M1
                        vector.wait_ge(s_mm, g + 1)
                        vector.tensor_tensor(
                            bufA[:, d2c, tc * 512:(tc + 1) * 512],
                            rT[:, d2c, tc * 512:(tc + 1) * 512],
                            M[:], AluOpType.mult).then_inc(s_pev, 1)
                for j in range(JQ):
                    gj = q * JQ + j
                    vector.wait_ge(s_tb, gj + 1)
                    if gj >= 2:
                        vector.wait_ge(s_st0 if gj % 2 == 0 else s_st1,
                                       16 * (gj // 2))
                    vector.tensor_tensor(
                        outb[:, j % 2, :], x3b[:, j, :],
                        TB[:].rearrange("p a b -> p (a b)"),
                        AluOpType.add).then_inc(s_add, 1)

    ph.close()
    return ph



def build_phase1(nc, x1_ap, x2_ap, gateT_ap, wg_ap, wu_ap, wd_ap, swg_ap,
                 swu_ap, swd_ap, ident_ap, tag="p1", n_experts=10,
                 debug=None, sem_stack=None, gcbuf=None):
    ph = Phase(nc, tag, sem_stack)
    QT = 1024
    NQ = S // QT
    JQ = QT // 128
    NI = 11
    NE = n_experts

    x1b = ph.sbuf("x1b", [128, 2, JQ, D], F32)
    xnbf = ph.sbuf("xnbf", [128, JQ, D], BF16)
    xnT = ph.sbuf("xnT", [128, NCC, QT], BF16)
    actT = ph.sbuf("actT", [128, NI, QT], BF16)
    bg = ph.sbuf("bg", [128, NCC, 1408], BF16)
    bu = ph.sbuf("bu", [128, NCC, 1408], BF16)
    bd = ph.sbuf("bd", [128, NI, D], BF16)
    gt = ph.sbuf("gt", [128, NCC, 8], BF16)
    sums = ph.sbuf("sums", [128, JQ], F32)
    negm = ph.sbuf("negm", [128, JQ], F32)
    ssq = ph.sbuf("ssq", [128, JQ], F32)
    rstd = ph.sbuf("rstd", [128, JQ], F32)
    sqdump = ph.sbuf("sqdump", [128, D], F32)
    ident = ph.sbuf("ident", [128, 128], BF16)
    cbuf = gcbuf if gcbuf is not None else ph.sbuf("cbuf", [128, 4], F32)
    ensure_const(nc, 1e-5, cbuf[:, 0:1])
    sgl = ph.sbuf("sgl", [128, 512], F32)
    tmp2 = ph.sbuf("tmp2", [128, 2, 512], F32)
    wbuf = ph.sbuf("wbuf", [128, JQ, 8], F32)
    gsc = ph.sbuf("gsc", [128, 8], F32)
    gcol = ph.sbuf("gcol", [128, 6], F32)
    g1 = ph.sbuf("g1", [128, 8], F32)
    g2 = ph.sbuf("g2", [128, 8], F32)
    g3 = ph.sbuf("g3", [128, 8], F32)
    gsg = ph.sbuf("gsg", [128, 8], F32)

    TP = ph.psum("TP", [128, 8, 128], BF16)
    GA = ph.psum("GA", [128, 512], F32)
    GB = ph.psum("GB", [128, 512], F32)
    C0 = ph.psum("C0", [128, 512], F32)
    C1 = ph.psum("C1", [128, 512], F32)
    LG = ph.psum("LG", [128, 8], F32)

    s_x1 = ph.sem("x1")
    s_bg = ph.sem("bg")
    s_bu = ph.sem("bu")
    s_bd = ph.sem("bd")
    s_c = ph.sem("c")
    s_gt = ph.sem("gt")
    s_k = ph.sem("k")
    s_red = ph.sem("red")
    s_stats = ph.sem("stats")
    s_xn = ph.sem("xn")
    s_tp = ph.sem("tp")
    s_ev = ph.sem("ev")
    s_act = ph.sem("act")
    s_gmm = ph.sem("gmm")
    s_gpre = ph.sem("gpre")
    s_gexp = ph.sem("gexp")
    s_gv = ph.sem("gv")
    s_gd1 = ph.sem("gd1")
    s_gsg1 = ph.sem("gsg1")
    s_gd2 = ph.sem("gd2")
    s_gsg2 = ph.sem("gsg2")
    s_gate = ph.sem("gate")
    s_mmgu = ph.sem("mmgu")
    s_silu = ph.sem("silu")
    s_fuse = ph.sem("fuse")
    s_mmd = ph.sem("mmd")
    s_ev2 = ph.sem("ev2")
    s_acc = ph.sem("acc")
    s_st = ph.sem("st")

    def esrc(e):
        if e < 8:
            return (wg_ap[e], wu_ap[e], wd_ap[e])
        h = e - 8
        return (swg_ap[:, h * 1408:(h + 1) * 1408],
                swu_ap[:, h * 1408:(h + 1) * 1408],
                swd_ap[h * 1408:(h + 1) * 1408, :])

    GU_PER_EQ = NI * 2 * 2
    D_PER_EQ = JQ * 2

    with nc.Block() as block:

        @block.sync
        def _(sync):
            sync.dma_start(out=ident[:], in_=ident_ap[:]).then_inc(s_c, 16)
            pass
            for q in range(NQ):
                if q >= 2:
                    sync.wait_ge(s_st, 32 * (q - 1))
                for h in range(2):
                    r0 = q * QT + h * 512
                    sync.dma_start(
                        out=x1b[:, q % 2, h * 4:(h + 1) * 4, :],
                        in_=x1_ap[r0:r0 + 512, :].rearrange(
                            "(j p) f -> p j f", p=128),
                    ).then_inc(s_x1, 16)
                if debug is not None and q == 0:
                    sync.wait_ge(s_ev, JQ)
                    sync.dma_start(out=debug["xnT"][:],
                                   in_=xnT[:]).then_inc(s_st, 16)
                    sync.wait_ge(s_gate, JQ)
                    sync.dma_start(out=debug["wbuf"][:],
                                   in_=wbuf[:]).then_inc(s_st, 16)
                    for nm, buf in [("gsc", gsc), ("g1", g1), ("g2", g2),
                                    ("g3", g3), ("gsg", gsg), ("gcol", gcol)]:
                        sync.dma_start(out=debug[nm][:],
                                       in_=buf[:]).then_inc(s_st, 16)
                    sync.wait_ge(s_fuse, NI * 2)
                    sync.dma_start(out=debug["actT"][:],
                                   in_=actT[:]).then_inc(s_st, 16)
                if q > 0:
                    sync.wait_ge(s_acc, NE * D_PER_EQ * q)
                    for h in range(2):
                        r0 = (q - 1) * QT + h * 512
                        sync.dma_start(
                            out=x2_ap[r0:r0 + 512, :].rearrange(
                                "(j p) f -> p j f", p=128),
                            in_=x1b[:, (q - 1) % 2, h * 4:(h + 1) * 4, :],
                        ).then_inc(s_st, 16)
            sync.wait_ge(s_acc, NE * D_PER_EQ * NQ)
            for h in range(2):
                r0 = (NQ - 1) * QT + h * 512
                sync.dma_start(
                    out=x2_ap[r0:r0 + 512, :].rearrange(
                        "(j p) f -> p j f", p=128),
                    in_=x1b[:, (NQ - 1) % 2, h * 4:(h + 1) * 4, :],
                ).then_inc(s_st, 16)
            sync.wait_ge(s_st, 16 * 2 * NQ + (144 if debug is not None else 0))

        @block.gpsimd
        def _(gpsimd):
            gpsimd.memset(cbuf[:, 0:1], 1e-5)
            gpsimd.memset(cbuf[:, 1:2], 0.0)
            gpsimd.memset(cbuf[:, 2:3], 1.0)
            gpsimd.memset(cbuf[:, 3:4], 1e-20).then_inc(s_k, 1)
            gpsimd.dma_start(out=gt[:], in_=gateT_ap.rearrange(
                "(c p) e -> p c e", p=128)).then_inc(s_gt, 16)
            n = 0
            for q in range(NQ):
                for e in range(NE):
                    sg, su, sd = esrc(e)
                    n += 1
                    if n > 1:
                        gpsimd.wait_ge(s_mmgu, GU_PER_EQ * (n - 1))
                    gpsimd.dma_start(out=bg[:], in_=sg.rearrange(
                        "(c p) f -> p c f", p=128)).then_inc(s_bg, 16)
                    gpsimd.dma_start(out=bu[:], in_=su.rearrange(
                        "(c p) f -> p c f", p=128)).then_inc(s_bu, 16)
                    if n > 1:
                        gpsimd.wait_ge(s_mmd, D_PER_EQ * (n - 1))
                    gpsimd.dma_start(out=bd[:], in_=sd.rearrange(
                        "(i p) f -> p i f", p=128)).then_inc(s_bd, 16)

        @block.tensor
        def _(tensor):
            tensor.wait_ge(s_c, 16)
            tensor.wait_ge(s_gt, 16)
            gu_g = 0
            d_g = 0
            for q in range(NQ):
                for j in range(JQ):
                    gj = q * JQ + j
                    tensor.wait_ge(s_xn, gj + 1)
                    if gj > 0:
                        tensor.wait_ge(s_ev, gj)
                    for cc in range(NCC):
                        inst = tensor.transpose(
                            TP[:, cc, :],
                            xnbf[:, j, cc * 128:(cc + 1) * 128],
                            ident[:])
                    inst.then_inc(s_tp, 1)
                for j in range(JQ):
                    gj = q * JQ + j
                    tensor.wait_ge(s_ev, gj + 1)
                    if gj > 0:
                        tensor.wait_ge(s_gate, gj)
                    for cc in range(NCC):
                        inst = tensor.matmul(
                            LG[:], xnT[:, cc, j * 128:(j + 1) * 128],
                            gt[:, cc, :],
                            start=(cc == 0), stop=(cc == NCC - 1))
                    inst.then_inc(s_gmm, 1)
                tensor.wait_ge(s_ev, JQ * (q + 1))
                for e in range(NE):
                    n = q * NE + e + 1
                    tensor.wait_ge(s_bg, 16 * n)
                    tensor.wait_ge(s_bu, 16 * n)
                    for ic in range(NI):
                        for tc2 in range(2):
                            for gu, (wt, MM) in enumerate([(bg, GA),
                                                           (bu, GB)]):
                                gu_g += 1
                                if gu == 0 and gu_g > 2:
                                    tensor.wait_ge(s_silu, (gu_g - 1) // 2)
                                if gu == 1 and gu_g > 2:
                                    tensor.wait_ge(s_fuse, (gu_g - 2) // 2)
                                for cc in range(NCC):
                                    inst = tensor.matmul(
                                        MM[:],
                                        wt[:, cc,
                                           ic * 128:(ic + 1) * 128],
                                        xnT[:, cc,
                                            tc2 * 512:(tc2 + 1) * 512],
                                        start=(cc == 0),
                                        stop=(cc == NCC - 1))
                                inst.then_inc(s_mmgu, 1)
                    tensor.wait_ge(s_bd, 16 * n)
                    tensor.wait_ge(s_fuse, NI * 2 * n)
                    for j in range(JQ):
                        for h2 in range(2):
                            d_g += 1
                            M = C0 if d_g % 2 == 1 else C1
                            if d_g > 2:
                                tensor.wait_ge(s_ev2, d_g - 2)
                            for ic in range(NI):
                                inst = tensor.matmul(
                                    M[:],
                                    actT[:, ic, j * 128:(j + 1) * 128],
                                    bd[:, ic, h2 * 512:(h2 + 1) * 512],
                                    start=(ic == 0), stop=(ic == NI - 1))
                            inst.then_inc(s_mmd, 1)

        @block.scalar
        def _(scalar):
            scalar.wait_ge(s_k, 1)
            actn = [0]

            def fence(inst):
                actn[0] += 1
                inst.then_inc(s_act, 1)
                scalar.wait_ge(s_act, actn[0])

            gu_g = 0
            d_g = 0
            for q in range(NQ):
                for j in range(JQ):
                    gj = q * JQ + j
                    scalar.wait_ge(s_red, gj + 1)
                    fence(scalar.mul(negm[:, j:j + 1], sums[:, j:j + 1],
                                     -1.0 / D))
                    fence(scalar.activation(sqdump[:], x1b[:, q % 2, j, :],
                                            AF.Square, bias=negm[:, j:j + 1],
                                            accum_out=ssq[:, j:j + 1]))
                    fence(scalar.mul(rstd[:, j:j + 1], ssq[:, j:j + 1],
                                     1.0 / D))
                    fence(scalar.add(rstd[:, j:j + 1], rstd[:, j:j + 1],
                                     1e-5))
                    fence(scalar.activation(rstd[:, j:j + 1],
                                            rstd[:, j:j + 1], AF.Ln))
                    scalar.activation(rstd[:, j:j + 1], rstd[:, j:j + 1],
                                      AF.Exp, scale=-0.5).then_inc(s_stats, 1)
                for j in range(JQ):
                    gj = q * JQ + j
                    scalar.wait_ge(s_gpre, gj + 1)
                    scalar.activation(gsc[:], LG[:], AF.Exp,
                                      bias=gcol[:, 1:2]).then_inc(s_gexp, 1)
                    scalar.wait_ge(s_gd1, gj + 1)
                    scalar.activation(gsg[:], g1[:],
                                      AF.Sign).then_inc(s_gsg1, 1)
                    scalar.wait_ge(s_gd2, gj + 1)
                    scalar.activation(gsg[:], g3[:],
                                      AF.Sign).then_inc(s_gsg2, 1)
                for e in range(NE):
                    for ic in range(NI):
                        for tc2 in range(2):
                            gu_g += 2
                            scalar.wait_ge(s_mmgu, gu_g - 1)
                            if gu_g > 2:
                                scalar.wait_ge(s_fuse, gu_g // 2 - 1)
                            scalar.activation(sgl[:], GA[:],
                                              AF.Silu).then_inc(s_silu, 1)
                    for j in range(JQ):
                        for h2 in range(2):
                            d_g += 1
                            M = C0 if d_g % 2 == 1 else C1
                            scalar.wait_ge(s_mmd, d_g)
                            if d_g > 2:
                                scalar.wait_ge(s_acc, d_g - 2)
                            if e < 8:
                                scalar.activation(
                                    tmp2[:, d_g % 2, :], M[:], AF.Identity,
                                    scale=wbuf[:, j, e:e + 1],
                                ).then_inc(s_ev2, 1)
                            else:
                                scalar.activation(
                                    tmp2[:, d_g % 2, :], M[:], AF.Identity,
                                ).then_inc(s_ev2, 1)

        @block.vector
        def _(vector):
            gu_g = 0
            d_g = 0
            gvn = [0]
            gvn_ms = [0]

            def gvn_sub(n):
                return n - gvn_ms[0]

            for q in range(NQ):
                for j in range(JQ):
                    vector.wait_ge(s_x1, 32 * (q + 1))
                    vector.reduce_sum(sums[:, j:j + 1],
                                      x1b[:, q % 2, j, :],
                                      axis=AX.X).then_inc(s_red, 1)
                for j in range(JQ):
                    gj = q * JQ + j
                    vector.wait_ge(s_stats, gj + 1)
                    vector.tensor_scalar(
                        xnbf[:, j, :], x1b[:, q % 2, j, :],
                        negm[:, j:j + 1],
                        rstd[:, j:j + 1], AluOpType.add, AluOpType.mult,
                    ).then_inc(s_xn, 1)
                for j in range(JQ):
                    gj = q * JQ + j
                    vector.wait_ge(s_tp, gj + 1)
                    vector.tensor_copy(
                        xnT[:, :, j * 128:(j + 1) * 128], TP[:],
                    ).then_inc(s_ev, 1)
                vector.wait_ge(s_k, 1)
                for j in range(JQ):
                    gj = q * JQ + j

                    def vf(inst, ms=None):
                        gvn[0] += 1
                        if ms is not None:
                            gvn_ms[0] += 1
                            inst.then_inc(ms, 1)
                        else:
                            inst.then_inc(s_gv, 1)

                    def vw():
                        vector.wait_ge(s_gv, gvn_sub(gvn[0]))

                    vector.wait_ge(s_gmm, gj + 1)
                    if gj > 0:
                        vector.wait_ge(s_gate, gj)
                        vector.wait_ge(s_gpre, gj)
                    vf(vector.reduce_max(gcol[:, 0:1], LG[:], axis=AX.X))
                    vw()
                    vf(vector.tensor_scalar(gcol[:, 1:2], gcol[:, 0:1], -1.0,
                                            None, AluOpType.mult), s_gpre)
                    vector.wait_ge(s_gexp, gj + 1)
                    vf(vector.reduce_sum(gcol[:, 2:3], gsc[:], axis=AX.X))
                    vw()
                    vf(vector.reciprocal(gcol[:, 2:3], gcol[:, 2:3]))
                    vw()
                    vf(vector.tensor_scalar(gsc[:], gsc[:], gcol[:, 2:3],
                                            None, AluOpType.mult))
                    vw()
                    vf(vector.reduce_max(gcol[:, 3:4], gsc[:], axis=AX.X))
                    vw()
                    vf(vector.tensor_scalar(g1[:], gsc[:], cbuf[:, 1:2],
                                            gcol[:, 3:4], AluOpType.mult,
                                            AluOpType.add))
                    vw()
                    vf(vector.tensor_tensor(g1[:], gsc[:], g1[:],
                                            AluOpType.subtract), s_gd1)
                    vector.wait_ge(s_gsg1, gj + 1)
                    vf(vector.tensor_scalar(g1[:], gsg[:], cbuf[:, 2:3],
                                            None, AluOpType.add))
                    vw()
                    vf(vector.tensor_scalar(g2[:], g1[:], 1e30, None,
                                            AluOpType.mult))
                    vw()
                    vf(vector.tensor_tensor(g2[:], gsc[:], g2[:],
                                            AluOpType.subtract))
                    vw()
                    vf(vector.reduce_max(gcol[:, 4:5], g2[:], axis=AX.X))
                    vw()
                    vf(vector.tensor_scalar(g3[:], gsc[:], cbuf[:, 1:2],
                                            gcol[:, 4:5], AluOpType.mult,
                                            AluOpType.add))
                    vw()
                    vf(vector.tensor_tensor(g3[:], g2[:], g3[:],
                                            AluOpType.subtract), s_gd2)
                    vector.wait_ge(s_gsg2, gj + 1)
                    vf(vector.tensor_scalar(g3[:], gsg[:], cbuf[:, 2:3],
                                            None, AluOpType.add))
                    vw()
                    vf(vector.tensor_tensor(g3[:], g1[:], g3[:],
                                            AluOpType.add))
                    vw()
                    vf(vector.tensor_tensor(g3[:], gsc[:], g3[:],
                                            AluOpType.mult))
                    vw()
                    vf(vector.tensor_tensor(gcol[:, 5:6], gcol[:, 3:4],
                                            gcol[:, 4:5], AluOpType.add))
                    vw()
                    vf(vector.tensor_scalar(gcol[:, 5:6], gcol[:, 5:6],
                                            cbuf[:, 3:4], None,
                                            AluOpType.add))
                    vw()
                    vf(vector.reciprocal(gcol[:, 5:6], gcol[:, 5:6]))
                    vw()
                    vf(vector.tensor_scalar(wbuf[:, j, :], g3[:],
                                            gcol[:, 5:6], None,
                                            AluOpType.mult), s_gate)
                for e in range(NE):
                    for ic in range(NI):
                        for tc2 in range(2):
                            gu_g += 2
                            vector.wait_ge(s_silu, gu_g // 2)
                            vector.wait_ge(s_mmgu, gu_g)
                            vector.tensor_tensor(
                                actT[:, ic, tc2 * 512:(tc2 + 1) * 512],
                                sgl[:], GB[:],
                                AluOpType.mult).then_inc(s_fuse, 1)
                    for j in range(JQ):
                        for h2 in range(2):
                            d_g += 1
                            vector.wait_ge(s_ev2, d_g)
                            vector.tensor_tensor(
                                x1b[:, q % 2, j, h2 * 512:(h2 + 1) * 512],
                                x1b[:, q % 2, j, h2 * 512:(h2 + 1) * 512],
                                tmp2[:, d_g % 2, :],
                                AluOpType.add).then_inc(s_acc, 1)

    ph.close()
    return ph




def build_attn(nc, layer, x_ap, out_ap, wq_ap, wk_ap, wv_ap, wo_ap,
               bqkv_ap, bo_ap, ident_ap, ones164_ap, ones128_ap,
               ones1128_ap, scr_q, scr_k, scr_v, scr_o, scr_c,
               pe_ap=None, h_ap=None, gb_ap=None,
               ropeC_ap=None, ropeS_ap=None, swap_ap=None, tag="pa",
               sem_stack=None, gcbuf=None):
    ph = Phase(nc, tag, sem_stack)
    QT = 1024
    NQ = S // QT
    JQ = QT // 128
    NP = 8

    AR1 = ph.sbuf("AR1", [128, 57408], BF16)
    hT_ = AR1[:, 0:32768].rearrange("p (c t) -> p c t", c=NCC)
    xnbf_ = AR1[:, 32768:40960].rearrange("p (j d) -> p j d", j=JQ)
    qT_ = AR1[:, 40960:45056]
    kT_ = AR1[:, 45056:49152]
    raw_ = AR1[:, 49152:53248]
    vag_ = AR1[:, 53248:57408].rearrange("p (k v) -> p k v", k=NT)
    cq_ = AR1[:, 0:8192].rearrange("p (s t) -> p s t", s=2)
    ck_ = AR1[:, 8192:16384].rearrange("p (s t) -> p s t", s=2)
    cv_ = AR1[:, 16384:24704].rearrange("p (s k v) -> p s k v", s=2, k=NT)
    onT_ = AR1[:, 24704:28800]
    pbuf_ = AR1[:, 28800:38016].rearrange("p (s q) -> p s q", s=18)
    wob_ = AR1[:, 38016:46208].rearrange("p (c f) -> p c f", c=NCC)
    olh_ = AR1[:, 46208:48256].rearrange("p (s c f) -> p s c f", s=2, c=NCC)

    xb = ph.sbuf("xb", [128, JQ, D], F32)
    peb = ph.sbuf("peb", [128, 2, D], F32) if layer == 0 else None
    sums = ph.sbuf("sums", [128, JQ], F32)
    negm = ph.sbuf("negm", [128, JQ], F32)
    ssq = ph.sbuf("ssq", [128, JQ], F32)
    rstd = ph.sbuf("rstd", [128, JQ], F32)
    sqdump = ph.sbuf("sqdump", [128, D], F32)
    ident = ph.sbuf("ident", [128, 128], BF16)
    ones164 = ph.sbuf("ones164", [1, 64], BF16)
    ones128 = ph.sbuf("ones128", [128, 1], BF16)
    ones1128 = ph.sbuf("ones1128", [1, 128], F32)
    cbuf = gcbuf if gcbuf is not None else ph.sbuf("cbuf", [128, 4], F32)
    ensure_const(nc, 1e-5, cbuf[:, 0:1])
    wqb = ph.sbuf("wqb", [128, NCC, 128], BF16)
    wkb = ph.sbuf("wkb", [128, NCC, 128], BF16)
    wvb = ph.sbuf("wvb", [128, NCC, 128], BF16)
    bqkv = ph.sbuf("bqkv", [128, NCC, 3], F32)
    sq = ph.sbuf("sq", [128, 512], BF16)
    nmx = ph.sbuf("nmx", [1, 8], F32)
    ncl = ph.sbuf("ncl", [1, 4], F32)
    negc = ph.sbuf("negc", [128, 1], F32)
    cnegc = ph.sbuf("cnegc", [128, 2], F32)
    if layer == 1:
        ropeC = ph.sbuf("ropeC", [128, S], BF16)
        ropeS = ph.sbuf("ropeS", [128, S], BF16)
        swap = ph.sbuf("swap", [128, 128], BF16)
    rt1 = ph.sbuf("rt1", [128, 512], BF16)
    aev = ph.sbuf("aev", [128, 512], BF16)
    recd = ph.sbuf("recd", [1, 512], BF16)
    bo = ph.sbuf("bo", [128, D], F32)
    gb = ph.sbuf("gb", [128, 2, D], F32) if layer == 0 else None
    hb = ph.sbuf("hb", [128, 2, D], F32)
    xr2 = ph.sbuf("xr2", [128, 2, D], F32) if layer == 0 else None
    outb = ph.sbuf("outb", [128, 2, D], F32)

    B01 = ph.psum("B01", [128, 1024], F32)
    B23 = ph.psum("B23", [128, 1024], F32)
    B4 = ph.psum("B4", [128, 512], F32)
    B5 = ph.psum("B5", [128, 512], F32)
    B6 = ph.psum("B6", [128, 512], F32)
    B7 = ph.psum("B7", [128, 8, 128], BF16)

    s_x = ph.sem("x")
    s_pe = ph.sem("pe") if layer == 0 else None
    s_red = ph.sem("red")
    s_stats = ph.sem("stats")
    s_xn = ph.sem("xn")
    s_tp = ph.sem("tp")
    s_ev = ph.sem("ev")
    s_act = ph.sem("act")
    s_vf = ph.sem("vf")
    s_hst = ph.sem("hst") if layer == 0 else None
    s_c = ph.sem("c")
    s_wp = ph.sem("wp")
    s_wo = ph.sem("wo")
    s_Bmm = ph.sem("Bmm")
    s_Bev = ph.sem("Bev")
    s_rot = ph.sem("rot")
    s_rope = ph.sem("rope")
    s_sqn = ph.sem("sqn")
    s_nmm = ph.sem("nmm")
    s_nred = ph.sem("nred")
    s_nprod = ph.sem("nprod")
    s_nsq = ph.sem("nsq")
    s_ngmm = ph.sem("ngmm")
    s_ngc = ph.sem("ngc")
    s_vtp = ph.sem("vtp")
    s_vev = ph.sem("vev")
    s_Bst = ph.sem("Bst")
    s_cl0 = ph.sem("cl0")
    s_cl1 = ph.sem("cl1")
    s_sc = ph.sem("sc")
    s_px = ph.sem("px")
    s_avm = ph.sem("avm")
    s_cev = ph.sem("cev")
    s_nrm = ph.sem("nrm")
    s_bc = ph.sem("bc")
    s_on = ph.sem("on")
    s_ost = ph.sem("ost")
    s_dl0 = ph.sem("dl0")
    s_dl1 = ph.sem("dl1")
    s_dmm = ph.sem("dmm")
    s_hr0 = ph.sem("hr0")
    s_hr1 = ph.sem("hr1")
    s_xr0 = ph.sem("xr0")
    s_xr1 = ph.sem("xr1")
    s_ep = ph.sem("ep")
    s_st0 = ph.sem("st0")
    s_st1 = ph.sem("st1")
    s_k = ph.sem("k")
    s_tab = ph.sem("tab")

    n_const = 6 + (1 if layer == 0 else 0)
    L1 = (layer == 1)

    with nc.Block() as block:

        @block.sync
        def _(sync):
            sync.dma_start(out=ident[:], in_=ident_ap[:]).then_inc(s_c, 16)
            sync.dma_start(out=ones164[:], in_=ones164_ap[:]).then_inc(s_c, 16)
            sync.dma_start(out=ones128[:], in_=ones128_ap[:]).then_inc(s_c, 16)
            sync.dma_start(out=ones1128[:], in_=ones1128_ap[:]).then_inc(s_c, 16)
            sync.dma_start(out=bqkv[:], in_=bqkv_ap[:]).then_inc(s_c, 16)
            sync.dma_start(out=bo[:], in_=bo_ap[:]).then_inc(s_c, 16)
            if layer == 0:
                sync.dma_start(out=gb[:], in_=gb_ap[:]).then_inc(s_c, 16)
            for q in range(NQ):
                for h2 in range(2):
                    r0 = q * QT + h2 * 512
                    sync.dma_start(
                        out=xb[:, h2 * 4:(h2 + 1) * 4, :],
                        in_=x_ap[r0:r0 + 512, :].rearrange(
                            "(j p) f -> p j f", p=128),
                    ).then_inc(s_x, 16)
                if layer == 0:
                    for j in range(JQ):
                        gj = q * JQ + j
                        sl = gj % 2
                        if gj >= 2:
                            sync.wait_ge(s_xn, gj - 1)
                        sync.dma_start(
                            out=peb[:, sl, :],
                            in_=pe_ap[gj * 128:(gj + 1) * 128, :],
                        ).then_inc(s_pe, 16)
                    sync.wait_ge(s_xn, JQ * (q + 1))
                    for h2 in range(2):
                        r0 = q * QT + h2 * 512
                        sync.dma_start(
                            out=h_ap[r0:r0 + 512, :].rearrange(
                                "(j p) f -> p j f", p=128),
                            in_=xb[:, h2 * 4:(h2 + 1) * 4, :],
                        ).then_inc(s_hst, 16)
                else:
                    sync.wait_ge(s_xn, JQ * (q + 1))
                if q < NQ - 1:
                    sync.wait_ge(s_ev, JQ * (q + 1))
                    if layer == 0:
                        sync.wait_ge(s_hst, 32 * (q + 1))
            for p in range(NP):
                if L1:
                    sync.wait_ge(s_rope, 16 * (p + 1))
                else:
                    sync.wait_ge(s_Bev, (p * 3 + 2) * 8)
                sync.dma_start(out=scr_q[:, p, :], in_=qT_).then_inc(
                    s_Bst, 16)
                sync.dma_start(out=scr_k[:, p, :], in_=kT_).then_inc(
                    s_Bst, 16)
                sync.wait_ge(s_vev, 4 * (p + 1))
                sync.dma_start(out=scr_v[:, p, :, :], in_=vag_).then_inc(
                    s_Bst, 16)
                sync.wait_ge(s_ngc, p + 1)
                sync.dma_start(out=scr_c[:, p:p + 1], in_=negc[:],
                               allow_slow_non_contiguous=True).then_inc(
                    s_Bst, 16)
            sync.wait_ge(s_Bst, 16 * 4 * NP)
            for p in range(NP):
                sl = p % 2
                s_cl = s_cl0 if sl == 0 else s_cl1
                if p >= 2:
                    sync.wait_ge(s_ost, 16 * (p - 1))
                sync.dma_start(out=cq_[:, sl, :],
                               in_=scr_q[:, p, :]).then_inc(s_cl, 16)
                sync.dma_start(out=ck_[:, sl, :],
                               in_=scr_k[:, p, :]).then_inc(s_cl, 16)
                sync.dma_start(out=cv_[:, sl, :, :],
                               in_=scr_v[:, p, :, :]).then_inc(s_cl, 16)
                sync.dma_start(out=cnegc[:, sl:sl + 1],
                               in_=scr_c[:, p:p + 1],
                               allow_slow_non_contiguous=True).then_inc(
                    s_cl, 16)
                if p > 0:
                    sync.wait_ge(s_on, 16 * p)
                    sync.dma_start(out=scr_o[:, p - 1, :],
                                   in_=onT_).then_inc(s_ost, 16)
            sync.wait_ge(s_on, 16 * NP)
            sync.dma_start(out=scr_o[:, NP - 1, :], in_=onT_).then_inc(
                s_ost, 16)
            for t in range(NT):
                sl = t % 2
                s_dl = s_dl0 if sl == 0 else s_dl1
                s_hr = s_hr0 if sl == 0 else s_hr1
                if t == 0:
                    sync.wait_ge(s_ost, 16 * NP)
                    if layer == 0:
                        sync.wait_ge(s_hst, 32 * NQ)
                if t >= 2:
                    sync.wait_ge(s_dmm, 2 * (t - 1))
                sync.dma_start(
                    out=olh_[:, sl, :, :],
                    in_=scr_o[:, :, t * 128:(t + 1) * 128],
                ).then_inc(s_dl, 16)
                if t >= 2:
                    sync.wait_ge(s_ep, t - 1)
                src_res = h_ap if layer == 0 else x_ap
                sync.dma_start(
                    out=hb[:, sl, :],
                    in_=src_res[t * 128:(t + 1) * 128, :],
                ).then_inc(s_hr, 16)
                if layer == 0:
                    s_xr = s_xr0 if sl == 0 else s_xr1
                    sync.dma_start(
                        out=xr2[:, sl, :],
                        in_=x_ap[t * 128:(t + 1) * 128, :],
                    ).then_inc(s_xr, 16)
                sync.wait_ge(s_ep, t + 1)
                sync.dma_start(
                    out=out_ap[t * 128:(t + 1) * 128, :],
                    in_=outb[:, t % 2, :],
                ).then_inc(s_st0 if t % 2 == 0 else s_st1, 16)
            sync.wait_ge(s_st0, 16 * (NT // 2))
            sync.wait_ge(s_st1, 16 * (NT // 2))

        @block.gpsimd
        def _(gpsimd):
            gpsimd.memset(cbuf[:, 0:1], 1e-5).then_inc(s_k, 1)
            gpsimd.memset(vag_[:, :, 64:65], 1.0)
            gpsimd.memset(vag_[:, :, 129:130], 1.0).then_inc(s_k, 1)
            if L1:
                gpsimd.dma_start(out=swap[:], in_=swap_ap[:]).then_inc(
                    s_tab, 16)
                gpsimd.dma_start(out=ropeC[:], in_=ropeC_ap[:]).then_inc(
                    s_tab, 16)
                gpsimd.dma_start(out=ropeS[:], in_=ropeS_ap[:]).then_inc(
                    s_tab, 16)
            for p in range(NP):
                if p > 0:
                    gpsimd.wait_ge(s_Bmm, 24 * p)
                for w_ap, wb in [(wq_ap, wqb), (wk_ap, wkb), (wv_ap, wvb)]:
                    gpsimd.dma_start(
                        out=wb[:],
                        in_=w_ap[:, p * 128:(p + 1) * 128].rearrange(
                            "(c p) f -> p c f", p=128),
                    ).then_inc(s_wp, 16)

            gpsimd.wait_ge(s_Bst, 64 * NP)
            gpsimd.dma_start(out=wob_[:, :, :], in_=wo_ap.rearrange(
                "(c p) f -> p c f", p=128)).then_inc(s_wo, 16)

        @block.tensor
        def _(tensor):
            tensor.wait_ge(s_c, 16 * n_const)
            tensor.wait_ge(s_k, 2)
            for q in range(NQ):
                for j in range(JQ):
                    gj = q * JQ + j
                    tensor.wait_ge(s_xn, gj + 1)
                    if gj > 0:
                        tensor.wait_ge(s_ev, gj)
                    for cc in range(NCC):
                        inst = tensor.transpose(
                            B7[:, cc, :],
                            xnbf_[:, j, cc * 128:(cc + 1) * 128],
                            ident[:])
                    inst.then_inc(s_tp, 1)
            for p in range(NP):
                tensor.wait_ge(s_wp, 48 * (p + 1))
                for mi in range(3):
                    wb = [wqb, wkb, wvb][mi]
                    for qc in range(8):
                        g = (p * 3 + mi) * 8 + qc + 1
                        Mv = B01[:, (g % 2) * 512:(g % 2) * 512 + 512]
                        tensor.wait_ge(s_ev, min(NT, 4 * (qc + 1)))
                        if g > 2:
                            tensor.wait_ge(s_Bev, g - 2)
                        for cc in range(NCC):
                            inst = tensor.matmul(
                                Mv, wb[:, cc, :],
                                hT_[:, cc, qc * 512:(qc + 1) * 512],
                                start=(cc == 0), stop=(cc == NCC - 1))
                        inst.then_inc(s_Bmm, 1)
                    if mi < 2:
                        if L1:
                            tensor.wait_ge(s_tab, 48)
                            if mi == 0 and p > 0:
                                tensor.wait_ge(s_ngc, p)
                            for qc in range(8):
                                rg = (p * 2 + mi) * 8 + qc + 1
                                tensor.wait_ge(
                                    s_Bev, (p * 3 + mi) * 8 + qc + 1)
                                if rg > 1:
                                    tensor.wait_ge(s_rope, rg - 1)
                                tensor.matmul(
                                    B6[:], swap[:],
                                    raw_[:, qc * 512:(qc + 1) * 512],
                                    start=True, stop=True).then_inc(s_rot, 1)
                        for qc in range(8):
                            n = (p * 2 + mi) * 8 + qc + 1
                            tensor.wait_ge(s_sqn, n)
                            if n > 1:
                                tensor.wait_ge(s_nred, n - 1)
                            tensor.matmul(
                                B23[0:1, 0:512], ones128[:], sq[:],
                                start=True, stop=True).then_inc(s_nmm, 1)
                        if mi == 1:
                            tensor.wait_ge(s_nsq, p + 1)
                            if L1:
                                tensor.wait_ge(s_rope, 16 * (p + 1))
                            elif p > 0:
                                tensor.wait_ge(s_ngc, p)
                            tensor.matmul(
                                B6[:, 0:1], ones1128[:], ncl[0:1, 3:4],
                                start=True, stop=True).then_inc(s_ngmm, 1)
                    else:
                        for tg in range(4):
                            tensor.wait_ge(
                                s_Bev, (p * 3 + 2) * 8 + 2 * tg + 2)
                            tensor.wait_ge(s_vev, 4 * p + tg)
                            for tt in range(8):
                                kc = tg * 8 + tt
                                inst = tensor.transpose(
                                    B7[:, tt, :],
                                    raw_[:, kc * 128:(kc + 1) * 128],
                                    ident[:])
                            inst.then_inc(s_vtp, 1)
            sc_g = 0
            bc_g = 0
            for p in range(NP):
                sl = p % 2
                s_cl = s_cl0 if sl == 0 else s_cl1
                tensor.wait_ge(s_cl, 64 * (p // 2 + 1))
                for qc in range(8):
                    aq = p * 8 + qc

                    def emit_av(kc, exp_cnt):
                        tensor.wait_ge(s_px, exp_cnt)
                        if kc == 0:
                            tensor.wait_ge(s_cev, 2 * aq)
                            tensor.wait_ge(s_nrm, 2 * aq)
                        u1 = exp_cnt - 1
                        tensor.matmul(
                            B4[0:65, :], cv_[:, sl, kc, 0:65],
                            pbuf_[:, (u1 - 1) % 18, :],
                            start=(kc == 0), stop=(kc == NT - 1))
                        tensor.matmul(
                            B5[0:65, :], cv_[:, sl, kc, 65:130],
                            pbuf_[:, u1 % 18, :],
                            start=(kc == 0), stop=(kc == NT - 1),
                        ).then_inc(s_avm, 1)

                    def emit_bcast():
                        nonlocal bc_g
                        bc_g += 1
                        tensor.wait_ge(s_nrm, bc_g)
                        if bc_g > 1:
                            tensor.wait_ge(s_on, bc_g - 1)
                        tensor.matmul(
                            B6[0:64, :], ones164[:], recd[:],
                            start=True, stop=True).then_inc(s_bc, 1)

                    SSb = [B01[:, 0:512], B01[:, 512:1024],
                           B23[:, 0:512], B23[:, 512:1024]]
                    for kc in range(NT):
                        for hh in range(2):
                            sc_g += 1
                            Mv = SSb[(sc_g - 1) % 4]
                            if sc_g > 4:
                                tensor.wait_ge(s_px, sc_g - 4)
                            tensor.matmul(
                                Mv,
                                ck_[hh * 64:hh * 64 + 64, sl,
                                    kc * 128:(kc + 1) * 128],
                                cq_[hh * 64:hh * 64 + 64, sl,
                                    qc * 512:(qc + 1) * 512],
                                start=True, stop=True,
                                tile_position=(hh * 64, 0)).then_inc(s_sc, 1)
                        if kc >= 7:
                            emit_av(kc - 7, sc_g - 14)
                        if aq > 0 and kc == 5:
                            emit_bcast()
                        if aq > 0 and kc == 6:
                            emit_bcast()
                    for kcl in range(NT - 7, NT):
                        emit_av(kcl, sc_g - 2 * (NT - 1 - kcl))
            for _ in range(2):
                bc_g += 1
                tensor.wait_ge(s_nrm, bc_g)
                tensor.wait_ge(s_on, bc_g - 1)
                tensor.matmul(
                    B6[0:64, :], ones164[:], recd[:],
                    start=True, stop=True).then_inc(s_bc, 1)
            tensor.wait_ge(s_wo, 16)
            for t in range(NT):
                sl = t % 2
                s_dl = s_dl0 if sl == 0 else s_dl1
                tensor.wait_ge(s_dl, 16 * (t // 2 + 1))
                if t >= 2:
                    tensor.wait_ge(s_ep, t - 1)
                Pd = B01 if t % 2 == 0 else B23
                for h2 in range(2):
                    Mv = Pd[:, h2 * 512:(h2 + 1) * 512]
                    for cc in range(NCC):
                        inst = tensor.matmul(
                            Mv, olh_[:, sl, cc, :],
                            wob_[:, cc, h2 * 512:(h2 + 1) * 512],
                            start=(cc == 0), stop=(cc == NCC - 1))
                    inst.then_inc(s_dmm, 1)

        @block.scalar
        def _(scalar):
            scalar.wait_ge(s_k, 1)
            actn = [0]

            def fence(inst):
                actn[0] += 1
                inst.then_inc(s_act, 1)
                scalar.wait_ge(s_act, actn[0])

            for q in range(NQ):
                for j in range(JQ):
                    gj = q * JQ + j
                    scalar.wait_ge(s_red, gj + 1)
                    fence(scalar.mul(negm[:, j:j + 1], sums[:, j:j + 1],
                                     -1.0 / D))
                    fence(scalar.activation(sqdump[:], xb[:, j, :],
                                            AF.Square, bias=negm[:, j:j + 1],
                                            accum_out=ssq[:, j:j + 1]))
                    fence(scalar.mul(rstd[:, j:j + 1], ssq[:, j:j + 1],
                                     1.0 / D))
                    fence(scalar.add(rstd[:, j:j + 1], rstd[:, j:j + 1],
                                     1e-5))
                    fence(scalar.activation(rstd[:, j:j + 1],
                                            rstd[:, j:j + 1], AF.Ln))
                    scalar.activation(rstd[:, j:j + 1], rstd[:, j:j + 1],
                                      AF.Exp, scale=-0.5).then_inc(s_stats, 1)
            for p in range(NP):
                for mi in range(3):
                    for qc in range(8):
                        g = (p * 3 + mi) * 8 + qc + 1
                        Mv = B01[:, (g % 2) * 512:(g % 2) * 512 + 512]
                        scalar.wait_ge(s_Bmm, g)
                        if p > 0 and mi == 0 and qc == 0:
                            scalar.wait_ge(s_Bst, 64 * p)
                        if L1 and mi < 2:
                            dst = raw_
                        elif mi == 0:
                            dst = qT_
                        elif mi == 1:
                            dst = kT_
                        else:
                            dst = raw_
                        if L1 and mi == 1:
                            scalar.wait_ge(s_rope, p * 16 + qc + 1)
                        if L1 and mi == 2:
                            scalar.wait_ge(s_rope, p * 16 + 8 + qc + 1)
                        scalar.activation(
                            dst[:, qc * 512:(qc + 1) * 512], Mv,
                            AF.Identity,
                            bias=bqkv[:, p, mi:mi + 1]).then_inc(s_Bev, 1)
                    if mi < 2:
                        src = qT_ if mi == 0 else kT_
                        for qc in range(8):
                            n = (p * 2 + mi) * 8 + qc + 1
                            if L1:
                                scalar.wait_ge(s_rope, (p * 2 + mi) * 8
                                               + qc + 1)
                            else:
                                scalar.wait_ge(s_Bev,
                                               (p * 3 + mi) * 8 + qc + 1)
                            if n > 1:
                                scalar.wait_ge(s_nmm, n - 1)
                            scalar.activation(
                                sq[:], src[:, qc * 512:(qc + 1) * 512],
                                AF.Square).then_inc(s_sqn, 1)
                        if mi == 1:
                            scalar.wait_ge(s_nprod, p + 1)
                            fence(scalar.activation(ncl[0:1, 3:4],
                                                    ncl[0:1, 2:3], AF.Ln))
                            fence(scalar.activation(ncl[0:1, 3:4],
                                                    ncl[0:1, 3:4],
                                                    AF.Exp, scale=0.5))
                            scalar.mul(ncl[0:1, 3:4], ncl[0:1, 3:4],
                                       -0.125).then_inc(s_nsq, 1)
            px_g = 0
            cev_g = 0
            for p in range(NP):
                sl = p % 2
                s_cl = s_cl0 if sl == 0 else s_cl1
                scalar.wait_ge(s_cl, 64 * (p // 2 + 1))
                SSb = [B01[:, 0:512], B01[:, 512:1024],
                       B23[:, 0:512], B23[:, 512:1024]]

                def emit_cev(aq_done, hh):
                    nonlocal cev_g
                    cev_g += 1
                    scalar.wait_ge(s_avm, 32 * (aq_done + 1))
                    if cev_g > 1:
                        scalar.wait_ge(s_on, cev_g - 1)
                    AV = B4 if hh == 0 else B5
                    scalar.activation(aev[0:64, :], AV[0:64, :],
                                      AF.Identity).then_inc(s_cev, 1)

                for qc in range(8):
                    aq = p * 8 + qc
                    for kc in range(NT):
                        for hh in range(2):
                            px_g += 1
                            scalar.wait_ge(s_sc, px_g)
                            if px_g > 18:
                                scalar.wait_ge(s_avm, (px_g - 17) // 2)
                            scalar.activation(
                                pbuf_[:, (px_g - 1) % 18, :],
                                SSb[(px_g - 1) % 4], AF.Exp, scale=0.125,
                                bias=cnegc[:, sl:sl + 1]).then_inc(s_px, 1)
                        if aq > 0 and kc == 3:
                            emit_cev(aq - 1, 0)
                        if aq > 0 and kc == 4:
                            emit_cev(aq - 1, 1)
                if p == NP - 1:
                    emit_cev(p * 8 + 7, 0)
                    emit_cev(p * 8 + 7, 1)
            if layer == 0:
                for t in range(NT):
                    c = t % 2
                    scalar.wait_ge(s_red, NT + t + 1)
                    if t > 1:
                        scalar.wait_ge(s_ep, t - 1)
                    fence(scalar.mul(negm[:, c:c + 1], sums[:, c:c + 1],
                                     -1.0 / D))
                    fence(scalar.activation(sqdump[:], outb[:, c, :],
                                            AF.Square, bias=negm[:, c:c + 1],
                                            accum_out=ssq[:, c:c + 1]))
                    fence(scalar.mul(rstd[:, c:c + 1], ssq[:, c:c + 1],
                                     1.0 / D))
                    fence(scalar.add(rstd[:, c:c + 1], rstd[:, c:c + 1],
                                     1e-5))
                    fence(scalar.activation(rstd[:, c:c + 1],
                                            rstd[:, c:c + 1], AF.Ln))
                    scalar.activation(rstd[:, c:c + 1], rstd[:, c:c + 1],
                                      AF.Exp, scale=-0.5).then_inc(s_stats, 1)

        @block.vector
        def _(vector):
            vector.wait_ge(s_k, 2)
            vfn = [0]

            def vfence(inst):
                vfn[0] += 1
                inst.then_inc(s_vf, 1)
                vector.wait_ge(s_vf, vfn[0])

            for q in range(NQ):
                for j in range(JQ):
                    vector.wait_ge(s_x, 32 * (q + 1))
                    vector.reduce_sum(sums[:, j:j + 1], xb[:, j, :],
                                      axis=AX.X).then_inc(s_red, 1)
                for j in range(JQ):
                    gj = q * JQ + j
                    vector.wait_ge(s_stats, gj + 1)
                    if layer == 0:
                        vector.wait_ge(s_pe, 16 * (gj + 1))
                        vector.tensor_scalar(
                            xb[:, j, :], xb[:, j, :], negm[:, j:j + 1],
                            rstd[:, j:j + 1], AluOpType.add, AluOpType.mult)
                        vector.tensor_tensor(
                            xb[:, j, :], xb[:, j, :], peb[:, gj % 2, :],
                            AluOpType.add)
                        vector.tensor_copy(
                            xnbf_[:, j, :], xb[:, j, :]).then_inc(s_xn, 1)
                    else:
                        vector.tensor_scalar(
                            xnbf_[:, j, :], xb[:, j, :], negm[:, j:j + 1],
                            rstd[:, j:j + 1], AluOpType.add,
                            AluOpType.mult).then_inc(s_xn, 1)
                for j in range(JQ):
                    gj = q * JQ + j
                    vector.wait_ge(s_tp, gj + 1)
                    vector.tensor_copy(
                        hT_[:, :, gj * 128:(gj + 1) * 128], B7[:],
                    ).then_inc(s_ev, 1)
            for p in range(NP):
                for mi in range(2):
                    if L1:
                        dst = qT_ if mi == 0 else kT_
                        vector.wait_ge(s_tab, 48)
                        for qc in range(8):
                            rg = (p * 2 + mi) * 8 + qc + 1
                            vector.wait_ge(s_rot, rg)
                            vector.tensor_tensor(
                                rt1[:], raw_[:, qc * 512:(qc + 1) * 512],
                                ropeC[:, qc * 512:(qc + 1) * 512],
                                AluOpType.mult)
                            vector.tensor_tensor(
                                dst[:, qc * 512:(qc + 1) * 512], B6[:],
                                ropeS[:, qc * 512:(qc + 1) * 512],
                                AluOpType.mult)
                            vector.tensor_tensor(
                                dst[:, qc * 512:(qc + 1) * 512],
                                dst[:, qc * 512:(qc + 1) * 512], rt1[:],
                                AluOpType.add).then_inc(s_rope, 1)
                    for qc in range(8):
                        n = (p * 2 + mi) * 8 + qc + 1
                        vector.wait_ge(s_nmm, n)
                        vector.reduce_max(nmx[0:1, qc:qc + 1],
                                          B23[0:1, 0:512],
                                          axis=AX.X).then_inc(s_nred, 1)
                    vector.wait_ge(s_nred, (p * 2 + mi) * 8 + 8)
                    vfence(vector.reduce_max(ncl[0:1, mi:mi + 1],
                                             nmx[0:1, 0:8], axis=AX.X))
                vector.tensor_tensor(ncl[0:1, 2:3], ncl[0:1, 0:1],
                                     ncl[0:1, 1:2],
                                     AluOpType.mult).then_inc(s_nprod, 1)
                vector.wait_ge(s_ngmm, p + 1)
                if p > 0:
                    vector.wait_ge(s_Bst, 64 * p)
                vector.tensor_copy(negc[:], B6[:, 0:1]).then_inc(s_ngc, 1)
                for tg in range(4):
                    vector.wait_ge(s_vtp, 4 * p + tg + 1)
                    if p > 0 and tg == 0:
                        vector.wait_ge(s_Bst, 64 * p)
                    vector.tensor_copy(
                        vag_[:, tg * 8:(tg + 1) * 8, 0:64], B7[:, :, 0:64])
                    vector.tensor_copy(
                        vag_[:, tg * 8:(tg + 1) * 8, 65:129],
                        B7[:, :, 64:128]).then_inc(s_vev, 1)
            on_g = 0
            for p in range(NP):
                for qc in range(8):
                    aq = p * 8 + qc
                    for hh in range(2):
                        AV = B4 if hh == 0 else B5
                        vector.wait_ge(s_cev, 2 * aq + hh + 1)
                        if 2 * aq + hh > 0:
                            vector.wait_ge(s_bc, 2 * aq + hh)
                        with nc.allow_low_precision(
                                reason="bf16 softmax denom for bcast matmul"):
                            vector.reciprocal(recd[:],
                                              AV[64:65, :]).then_inc(s_nrm, 1)
                        on_g += 1
                        vector.wait_ge(s_bc, on_g)
                        if on_g == 1 and p == 0:
                            pass
                        if qc == 0 and hh == 0 and p > 0:
                            vector.wait_ge(s_ost, 16 * p)
                        vector.tensor_tensor(
                            onT_.rearrange("p (h q) -> p h q", h=8)[
                                hh * 64:(hh + 1) * 64, qc // 1, :]
                            if False else
                            onT_[hh * 64:(hh + 1) * 64,
                                 qc * 512:(qc + 1) * 512],
                            aev[0:64, :], B6[0:64, :],
                            AluOpType.mult).then_inc(s_on, 1)
            for t in range(NT):
                sl = t % 2
                s_hr = s_hr0 if sl == 0 else s_hr1
                vector.wait_ge(s_dmm, 2 * (t + 1))
                vector.wait_ge(s_hr, 16 * (t // 2 + 1))
                if t >= 2:
                    vector.wait_ge(s_st0 if t % 2 == 0 else s_st1,
                                   16 * (t // 2))
                Pd = B01 if t % 2 == 0 else B23
                c = t % 2
                yb = outb[:, c, :]
                if layer == 0:
                    s_xr = s_xr0 if sl == 0 else s_xr1
                    vector.wait_ge(s_xr, 16 * (t // 2 + 1))
                    vector.tensor_tensor(yb, hb[:, sl, :], Pd[:],
                                         AluOpType.add)
                    vector.tensor_tensor(yb, yb, bo[:], AluOpType.add)
                    vector.reduce_sum(sums[:, c:c + 1], yb,
                                      axis=AX.X).then_inc(s_red, 1)
                    vector.wait_ge(s_stats, NT + t + 1)
                    vector.tensor_scalar(yb, yb, negm[:, c:c + 1],
                                         rstd[:, c:c + 1], AluOpType.add,
                                         AluOpType.mult)
                    vector.tensor_tensor(yb, yb, gb[:, 0, :],
                                         AluOpType.mult)
                    vector.tensor_tensor(yb, yb, gb[:, 1, :], AluOpType.add)
                    vector.tensor_tensor(yb, xr2[:, sl, :], yb,
                                         AluOpType.add).then_inc(s_ep, 1)
                else:
                    vector.tensor_tensor(yb, hb[:, sl, :], Pd[:],
                                         AluOpType.add)
                    vector.tensor_tensor(yb, yb, bo[:],
                                         AluOpType.add).then_inc(s_ep, 1)

    ph.close()
    return ph



def _attn_host_inputs(inputs, pfx, consts):
    bqkv = np.zeros((128, 8, 3), np.float32)
    for mi, nm in enumerate(["bq", "bk", "bv"]):
        bqkv[:, :, mi] = inputs[pfx + nm].reshape(8, 128).T
    bo_rep = np.ascontiguousarray(np.broadcast_to(
        inputs[pfx + "bo"][None, :].astype(np.float32), (128, D)))
    base = {
        "wq": inputs[pfx + "Wq"], "wk": inputs[pfx + "Wk"],
        "wv": inputs[pfx + "Wv"], "wo": inputs[pfx + "Wo"],
        "bqkv": bqkv, "bo": bo_rep, "ident": consts["ident"],
        "o164": np.ones((1, 64), dtype=BF),
        "o128": np.ones((128, 1), dtype=BF),
        "o1128": np.ones((1, 128), np.float32),
    }
    return base


def _declare_attn_tensors(nc, layer):
    t = {}
    t["x"] = nc.dram_tensor("x", [S, D], F32, kind="ExternalInput").ap()
    for nm in ["wq", "wk", "wv", "wo"]:
        t[nm] = nc.dram_tensor(nm, [D, D], F32, kind="ExternalInput").ap()
    t["bqkv"] = nc.dram_tensor("bqkv", [128, 8, 3], F32,
                               kind="ExternalInput").ap()
    t["bo"] = nc.dram_tensor("bo", [128, D], F32, kind="ExternalInput").ap()
    t["ident"] = nc.dram_tensor("ident", [128, 128], BF16,
                                kind="ExternalInput").ap()
    t["o164"] = nc.dram_tensor("o164", [1, 64], BF16,
                               kind="ExternalInput").ap()
    t["o128"] = nc.dram_tensor("o128", [128, 1], BF16,
                               kind="ExternalInput").ap()
    t["o1128"] = nc.dram_tensor("o1128", [1, 128], F32,
                                kind="ExternalInput").ap()
    t["scr_q"] = nc.dram_tensor("scr_q", [128, 8, S], BF16,
                                kind="Internal").ap()
    t["scr_k"] = nc.dram_tensor("scr_k", [128, 8, S], BF16,
                                kind="Internal").ap()
    t["scr_v"] = nc.dram_tensor("scr_v", [128, 8, 32, 130], BF16,
                                kind="Internal").ap()
    t["scr_o"] = nc.dram_tensor("scr_o", [128, 8, S], BF16,
                                kind="Internal").ap()
    t["scr_c"] = nc.dram_tensor("scr_c", [128, 8], F32,
                                kind="Internal").ap()
    t["xout"] = nc.dram_tensor("xout", [S, D], F32,
                               kind="ExternalOutput").ap()
    if layer == 0:
        t["pe"] = nc.dram_tensor("pe", [S, D], F32,
                                 kind="ExternalInput").ap()
        t["hscr"] = nc.dram_tensor("hscr", [S, D], F32, kind="Internal").ap()
        t["gb"] = nc.dram_tensor("gb", [128, 2, D], F32,
                                 kind="ExternalInput").ap()
    else:
        t["ropeC"] = nc.dram_tensor("ropeC", [128, S], F32,
                                    kind="ExternalInput").ap()
        t["ropeS"] = nc.dram_tensor("ropeS", [128, S], F32,
                                    kind="ExternalInput").ap()
        t["swapP"] = nc.dram_tensor("swapP", [128, 128], BF16,
                                    kind="ExternalInput").ap()
    return t


def kernel(**inputs):
    if FUSED:
        return _kernel_fused(**inputs)
    return _kernel_unfused(**inputs)


FUSED = True


def _kernel_fused(**inputs):
    consts = host_constants()
    nc = bass.Bass("TRN2", target_bir_lowering=False)

    EI, EO, IN = "ExternalInput", "ExternalOutput", "Internal"
    t = {}
    t["x"] = nc.dram_tensor("x", [S, D], F32, kind=EI).ap()
    for L, pfx in [(0, "a0"), (1, "a1")]:
        for nm in ["wq", "wk", "wv", "wo"]:
            t[pfx + nm] = nc.dram_tensor(pfx + nm, [D, D], F32, kind=EI).ap()
        t[pfx + "bqkv"] = nc.dram_tensor(pfx + "bqkv", [128, 8, 3], F32,
                                         kind=EI).ap()
        t[pfx + "bo"] = nc.dram_tensor(pfx + "bo", [128, D], F32,
                                       kind=EI).ap()
    t["ident"] = nc.dram_tensor("ident", [128, 128], BF16, kind=EI).ap()
    t["o164"] = nc.dram_tensor("o164", [1, 64], BF16, kind=EI).ap()
    t["o128"] = nc.dram_tensor("o128", [128, 1], BF16, kind=EI).ap()
    t["o1128"] = nc.dram_tensor("o1128", [1, 128], F32, kind=EI).ap()
    t["pe"] = nc.dram_tensor("pe", [S, D], F32, kind=EI).ap()
    t["gb"] = nc.dram_tensor("gb", [128, 2, D], F32, kind=EI).ap()
    t["ropeC"] = nc.dram_tensor("ropeC", [128, S], F32, kind=EI).ap()
    t["ropeS"] = nc.dram_tensor("ropeS", [128, S], F32, kind=EI).ap()
    t["swapP"] = nc.dram_tensor("swapP", [128, 128], BF16, kind=EI).ap()
    t["gateT"] = nc.dram_tensor("gateT", [D, 8], F32, kind=EI).ap()
    t["wg"] = nc.dram_tensor("wg", [8, D, 1408], F32, kind=EI).ap()
    t["wu"] = nc.dram_tensor("wu", [8, D, 1408], F32, kind=EI).ap()
    t["wd"] = nc.dram_tensor("wd", [8, 1408, D], F32, kind=EI).ap()
    t["swg"] = nc.dram_tensor("swg", [D, 2816], F32, kind=EI).ap()
    t["swu"] = nc.dram_tensor("swu", [D, 2816], F32, kind=EI).ap()
    t["swd"] = nc.dram_tensor("swd", [2816, D], F32, kind=EI).ap()
    t["wrT"] = nc.dram_tensor("wrT", [D, D], F32, kind=EI).ap()
    t["wkT"] = nc.dram_tensor("wkT", [D, D], F32, kind=EI).ap()
    t["wvT"] = nc.dram_tensor("wvT", [D, D], F32, kind=EI).ap()
    t["mur"] = nc.dram_tensor("mur", [D], F32, kind=EI).ap()
    t["muk"] = nc.dram_tensor("muk", [D], F32, kind=EI).ap()
    x1_d = nc.dram_tensor("x1s", [S, D], F32, kind=IN).ap()
    x2_d = nc.dram_tensor("x2s", [S, D], F32, kind=IN).ap()
    x3_d = nc.dram_tensor("x3s", [S, D], F32, kind=IN).ap()
    h_d = nc.dram_tensor("hscr", [S, D], F32, kind=IN).ap()
    scr = {}
    for nm, shp, dt in [("q", [128, 8, S], BF16), ("k", [128, 8, S], BF16),
                        ("v", [128, 8, 32, 130], BF16),
                        ("o", [128, 8, S], BF16), ("c", [128, 8], F32)]:
        scr[nm] = nc.dram_tensor("scr_" + nm, shp, dt, kind=IN).ap()
    out_d = nc.dram_tensor("out", [S, D], F32, kind=EO).ap()

    def free_phase_sems(ph):
        handles = list(ph.sems.values())
        with nc.Block() as blk:
            @blk.gpsimd
            def _(g):
                nc.clear_and_free_semaphores(handles)

    sem_leak = contextlib.ExitStack()
    with nc.sbuf_tensor("gcbuf", [128, 4], F32) as gcbuf:
        ph = build_attn(nc, 0, t["x"], x1_d, t["a0wq"], t["a0wk"],
                        t["a0wv"], t["a0wo"], t["a0bqkv"], t["a0bo"],
                        t["ident"], t["o164"], t["o128"], t["o1128"],
                        scr["q"], scr["k"], scr["v"], scr["o"], scr["c"],
                        pe_ap=t["pe"], h_ap=h_d, gb_ap=t["gb"], tag="pa0",
                        sem_stack=sem_leak, gcbuf=gcbuf)
        free_phase_sems(ph)
        ph = build_phase1(nc, x1_d, x2_d, t["gateT"], t["wg"], t["wu"],
                          t["wd"], t["swg"], t["swu"], t["swd"], t["ident"],
                          tag="p1", sem_stack=sem_leak, gcbuf=gcbuf)
        free_phase_sems(ph)
        ph = build_attn(nc, 1, x2_d, x3_d, t["a1wq"], t["a1wk"], t["a1wv"],
                        t["a1wo"], t["a1bqkv"], t["a1bo"], t["ident"],
                        t["o164"], t["o128"], t["o1128"], scr["q"], scr["k"],
                        scr["v"], scr["o"], scr["c"], ropeC_ap=t["ropeC"],
                        ropeS_ap=t["ropeS"], swap_ap=t["swapP"], tag="pa1",
                        sem_stack=sem_leak, gcbuf=gcbuf)
        free_phase_sems(ph)
        build_phase3(nc, x3_d, out_d, t["wrT"], t["wkT"], t["wvT"],
                     t["mur"], t["muk"], t["ident"], tag="p3",
                     sem_stack=sem_leak, gcbuf=gcbuf)

    base = {
        "ident": consts["ident"], "o164": np.ones((1, 64), dtype=BF),
        "o128": np.ones((128, 1), dtype=BF),
        "o1128": np.ones((1, 128), np.float32),
        "pe": consts["pe"], "ropeC": consts["ropeC"],
        "ropeS": consts["ropeS"], "swapP": consts["swapP"],
        "gateT": np.ascontiguousarray(inputs["L0_ffn_gate_w"].T),
        "wg": inputs["L0_ffn_wg"], "wu": inputs["L0_ffn_wu"],
        "wd": inputs["L0_ffn_wd"], "swg": inputs["L0_ffn_swg"],
        "swu": inputs["L0_ffn_swu"], "swd": inputs["L0_ffn_swd"],
        "wrT": np.ascontiguousarray(inputs["L1_ffn_W_r"].T),
        "wkT": np.ascontiguousarray(inputs["L1_ffn_W_k"].T),
        "wvT": np.ascontiguousarray(inputs["L1_ffn_W_v"].T),
        "mur": inputs["L1_ffn_mu_r"], "muk": inputs["L1_ffn_mu_k"],
    }
    for L, pfx, ipfx in [(0, "a0", "L0_attn_"), (1, "a1", "L1_attn_")]:
        bqkv = np.zeros((128, 8, 3), np.float32)
        for mi, nm in enumerate(["bq", "bk", "bv"]):
            bqkv[:, :, mi] = inputs[ipfx + nm].reshape(8, 128).T
        base[pfx + "bqkv"] = bqkv
        base[pfx + "bo"] = np.ascontiguousarray(np.broadcast_to(
            inputs[ipfx + "bo"][None, :].astype(np.float32), (128, D)))
        for nm in ["wq", "wk", "wv", "wo"]:
            base[pfx + nm] = inputs[ipfx + nm.capitalize()[0] + nm[1]]
    gbarr = np.stack([
        np.broadcast_to(inputs["L0_attn_gamma"][None, :], (128, D)),
        np.broadcast_to(inputs["L0_attn_beta"][None, :], (128, D))],
        axis=1).astype(np.float32)
    base["gb"] = np.ascontiguousarray(gbarr)

    x = inputs["L0_attn_x"]
    in_maps = [dict(base, x=np.ascontiguousarray(x[c])) for c in range(NC)]
    res = run_bass_kernel_spmd(nc, in_maps, core_ids=list(range(NC)))
    out = np.stack([res.results[c]["out"] for c in range(NC)])
    return out.astype(np.float32)


def _kernel_unfused(**inputs):
    consts = host_constants()

    nc0 = bass.Bass("TRN2", target_bir_lowering=False)
    t0 = _declare_attn_tensors(nc0, 0)
    build_attn(nc0, 0, t0["x"], t0["xout"], t0["wq"], t0["wk"], t0["wv"],
               t0["wo"], t0["bqkv"], t0["bo"], t0["ident"], t0["o164"],
               t0["o128"], t0["o1128"], t0["scr_q"], t0["scr_k"], t0["scr_v"],
               t0["scr_o"], t0["scr_c"], pe_ap=t0["pe"], h_ap=t0["hscr"],
               gb_ap=t0["gb"], tag="pa")
    base0 = _attn_host_inputs(inputs, "L0_attn_", consts)
    base0["pe"] = consts["pe"]
    gb = np.stack([
        np.broadcast_to(inputs["L0_attn_gamma"][None, :], (128, D)),
        np.broadcast_to(inputs["L0_attn_beta"][None, :], (128, D))],
        axis=1).astype(np.float32)
    base0["gb"] = np.ascontiguousarray(gb)
    x = inputs["L0_attn_x"]
    in_maps = [dict(base0, x=np.ascontiguousarray(x[c])) for c in range(NC)]
    res = run_bass_kernel_spmd(nc0, in_maps, core_ids=list(range(NC)))
    x1 = [res.results[c]["xout"] for c in range(NC)]

    nc1 = bass.Bass("TRN2", target_bir_lowering=False)
    x1_t = nc1.dram_tensor("x1", [S, D], F32, kind="ExternalInput").ap()
    gateT_t = nc1.dram_tensor("gateT", [D, 8], F32,
                              kind="ExternalInput").ap()
    wg_t = nc1.dram_tensor("wg", [8, D, 1408], F32,
                           kind="ExternalInput").ap()
    wu_t = nc1.dram_tensor("wu", [8, D, 1408], F32,
                           kind="ExternalInput").ap()
    wd_t = nc1.dram_tensor("wd", [8, 1408, D], F32,
                           kind="ExternalInput").ap()
    swg_t = nc1.dram_tensor("swg", [D, 2816], F32, kind="ExternalInput").ap()
    swu_t = nc1.dram_tensor("swu", [D, 2816], F32, kind="ExternalInput").ap()
    swd_t = nc1.dram_tensor("swd", [2816, D], F32, kind="ExternalInput").ap()
    ident1_t = nc1.dram_tensor("ident", [128, 128], BF16,
                               kind="ExternalInput").ap()
    x2_t = nc1.dram_tensor("x2", [S, D], F32, kind="ExternalOutput").ap()
    build_phase1(nc1, x1_t, x2_t, gateT_t, wg_t, wu_t, wd_t, swg_t, swu_t,
                 swd_t, ident1_t)
    base1 = {
        "gateT": np.ascontiguousarray(inputs["L0_ffn_gate_w"].T),
        "wg": inputs["L0_ffn_wg"], "wu": inputs["L0_ffn_wu"],
        "wd": inputs["L0_ffn_wd"], "swg": inputs["L0_ffn_swg"],
        "swu": inputs["L0_ffn_swu"], "swd": inputs["L0_ffn_swd"],
        "ident": consts["ident"],
    }
    in_maps = [dict(base1, x1=x1[c]) for c in range(NC)]
    res = run_bass_kernel_spmd(nc1, in_maps, core_ids=list(range(NC)))
    x2 = [res.results[c]["x2"] for c in range(NC)]

    nc2 = bass.Bass("TRN2", target_bir_lowering=False)
    t2 = _declare_attn_tensors(nc2, 1)
    build_attn(nc2, 1, t2["x"], t2["xout"], t2["wq"], t2["wk"], t2["wv"],
               t2["wo"], t2["bqkv"], t2["bo"], t2["ident"], t2["o164"],
               t2["o128"], t2["o1128"], t2["scr_q"], t2["scr_k"], t2["scr_v"],
               t2["scr_o"], t2["scr_c"], ropeC_ap=t2["ropeC"],
               ropeS_ap=t2["ropeS"], swap_ap=t2["swapP"], tag="pa")
    base2 = _attn_host_inputs(inputs, "L1_attn_", consts)
    base2["ropeC"] = consts["ropeC"]
    base2["ropeS"] = consts["ropeS"]
    base2["swapP"] = consts["swapP"]
    in_maps = [dict(base2, x=x2[c]) for c in range(NC)]
    res = run_bass_kernel_spmd(nc2, in_maps, core_ids=list(range(NC)))
    x3 = [res.results[c]["xout"] for c in range(NC)]

    nc3 = bass.Bass("TRN2", target_bir_lowering=False)
    x3_t = nc3.dram_tensor("x3", [S, D], F32, kind="ExternalInput").ap()
    wrT_t = nc3.dram_tensor("wrT", [D, D], F32, kind="ExternalInput").ap()
    wkT_t = nc3.dram_tensor("wkT", [D, D], F32, kind="ExternalInput").ap()
    wvT_t = nc3.dram_tensor("wvT", [D, D], F32, kind="ExternalInput").ap()
    mur_t = nc3.dram_tensor("mur", [D], F32, kind="ExternalInput").ap()
    muk_t = nc3.dram_tensor("muk", [D], F32, kind="ExternalInput").ap()
    ident3_t = nc3.dram_tensor("ident", [128, 128], BF16,
                               kind="ExternalInput").ap()
    out_t = nc3.dram_tensor("out", [S, D], F32, kind="ExternalOutput").ap()
    build_phase3(nc3, x3_t, out_t, wrT_t, wkT_t, wvT_t, mur_t, muk_t,
                 ident3_t)
    base3 = {
        "wrT": np.ascontiguousarray(inputs["L1_ffn_W_r"].T),
        "wkT": np.ascontiguousarray(inputs["L1_ffn_W_k"].T),
        "wvT": np.ascontiguousarray(inputs["L1_ffn_W_v"].T),
        "mur": inputs["L1_ffn_mu_r"], "muk": inputs["L1_ffn_mu_k"],
        "ident": consts["ident"],
    }
    in_maps = [dict(base3, x3=x3[c]) for c in range(NC)]
    res = run_bass_kernel_spmd(nc3, in_maps, core_ids=list(range(NC)))
    out = np.stack([res.results[c]["out"] for c in range(NC)])
    return out.astype(np.float32)


# version 13
# speedup vs baseline: 1.7819x; 1.0367x over previous
import contextlib
import math

import numpy as np
import ml_dtypes
import concourse.bass as bass
import concourse.mybir as mybir
from concourse.bass_utils import run_bass_kernel_spmd
from concourse.alu_op_type import AluOpType

F32 = mybir.dt.float32
BF16 = mybir.dt.bfloat16
AF = mybir.ActivationFunctionType
AX = mybir.AxisListType

S = 4096
D = 1024
NC = 8
NH = 16
DH = 64
NT = S // 128
NCC = D // 128

BF = ml_dtypes.bfloat16



def host_constants():
    c = {}
    c["ident"] = np.eye(128, dtype=BF)
    pos = np.arange(S, dtype=np.float32)[:, None]
    div = np.exp(np.arange(0, D, 2, dtype=np.float32)
                 * np.float32(-np.log(10000.0) / D))
    pe = np.zeros((S, D), dtype=np.float32)
    pe[:, 0::2] = np.sin(pos * div)
    pe[:, 1::2] = np.cos(pos * div)
    c["pe"] = pe
    inv_freq = (1.0 / 10000.0 ** (np.arange(0, DH, 2, dtype=np.float32) / DH))
    t = np.arange(S, dtype=np.float32)
    freqs = t[:, None] * inv_freq[None, :]
    rope = np.concatenate((freqs, freqs), axis=-1)
    ang = rope[:, ::2]
    cos, sin = np.cos(ang), np.sin(ang)
    Cfull = np.empty((S, DH), dtype=np.float32)
    Sfull = np.empty((S, DH), dtype=np.float32)
    Cfull[:, 0::2] = cos
    Cfull[:, 1::2] = cos
    Sfull[:, 0::2] = -sin
    Sfull[:, 1::2] = sin
    c["ropeC"] = np.concatenate([Cfull.T, Cfull.T], axis=0).copy()
    c["ropeS"] = np.concatenate([Sfull.T, Sfull.T], axis=0).copy()
    P = np.zeros((128, 128), dtype=BF)
    for i in range(64):
        P[2 * i, 2 * i + 1] = 1
        P[2 * i + 1, 2 * i] = 1
    c["swapP"] = P
    return c



def ensure_const(nc, value, ap):
    reg = getattr(nc, "_const_reg", None)
    if reg is None:
        reg = set()
        nc._const_reg = reg
    if value not in reg:
        nc.const_aps.register(F32, value, ap)
        reg.add(value)
        return True
    return False


class Phase:

    def __init__(self, nc, tag, sem_stack=None):
        self.nc = nc
        self.tag = tag
        self.stack = contextlib.ExitStack()
        self.sem_stack = sem_stack if sem_stack is not None else self.stack
        self.sems = {}

    def sem(self, name):
        h = self.sem_stack.enter_context(
            self.nc.semaphore(f"{self.tag}_{name}"))
        self.sems[name] = h
        return h

    def sbuf(self, name, shape, dtype):
        return self.stack.enter_context(
            self.nc.sbuf_tensor(f"{self.tag}_{name}", shape, dtype))

    def psum(self, name, shape, dtype):
        return self.stack.enter_context(
            self.nc.psum_tensor(f"{self.tag}_{name}", shape, dtype))

    def close(self):
        self.stack.close()


def emit_ln_stats(vector, scalar, src2d, negm_col, rstd_col, sum_col, ssq_col,
                  sqdump, s_red, red_val, s_stats, eps=1e-5, dim=None):
    dim = dim or src2d.shape[-1]
    vector.reduce_sum(sum_col, src2d, axis=AX.X).then_inc(s_red, 1)
    scalar.wait_ge(s_red, red_val)
    scalar.mul(negm_col, sum_col, -1.0 / dim)
    scalar.activation(sqdump, src2d, AF.Square, bias=negm_col,
                      accum_out=ssq_col)
    scalar.activation(rstd_col, ssq_col, AF.Ln, scale=1.0 / dim, bias=eps)
    scalar.activation(rstd_col, rstd_col, AF.Exp, scale=-0.5).then_inc(
        s_stats, 1)



def build_phase3(nc, x3_ap, out_ap, wrT_ap, wkT_ap, wvT_ap, mur_ap, muk_ap,
                 ident_ap, tag="p3", sem_stack=None, gcbuf=None):
    ph = Phase(nc, tag, sem_stack)
    QT = 1024
    NQ = S // QT
    JQ = QT // 128
    TC = QT // 512

    x3b = ph.sbuf("x3b", [128, JQ, D], F32)
    xnT = ph.sbuf("xnT", [128, NCC, QT + 1], BF16)
    dbuf = ph.sbuf("dbuf", [128, NCC, QT], BF16)
    bufA = ph.sbuf("bufA", [128, NCC, QT], BF16)
    rT = ph.sbuf("rT", [128, NCC, QT], BF16)
    kactT = ph.sbuf("kactT", [128, NCC, QT], BF16)
    wr = ph.sbuf("wr", [128, NCC, D], BF16)
    wk = ph.sbuf("wk", [128, NCC, D], BF16)
    wv = ph.sbuf("wv", [128, NCC, D], BF16)
    mur = ph.sbuf("mur", [128, NCC], F32)
    muk = ph.sbuf("muk", [128, NCC], F32)
    ident = ph.sbuf("ident", [128, 128], BF16)
    xnbf = ph.sbuf("xnbf", [128, JQ, D], BF16)
    sums = ph.sbuf("sums", [128, JQ], F32)
    negm = ph.sbuf("negm", [128, JQ], F32)
    ssq = ph.sbuf("ssq", [128, JQ], F32)
    rstd = ph.sbuf("rstd", [128, JQ], F32)
    sqdump = ph.sbuf("sqdump", [128, D], F32)
    tbuf = ph.sbuf("tbuf", [128, QT], BF16)
    krelu = ph.sbuf("krelu", [128, 2, 512], BF16)
    kcol = ph.sbuf("kcol", [128, NCC, 1], BF16)
    outb = ph.sbuf("outb", [128, 2, D], F32)
    cbuf = gcbuf if gcbuf is not None else ph.sbuf("cbuf", [128, 4], F32)
    ensure_const(nc, 1e-5, cbuf[:, 0:1])

    TP = ph.psum("TP", [128, 8, 128], BF16)
    M0 = ph.psum("M0", [128, 512], F32)
    M1 = ph.psum("M1", [128, 512], F32)
    TB = ph.psum("TB", [128, 8, 128], BF16)

    s_x3 = ph.sem("x3")
    s_w = ph.sem("w")
    s_c = ph.sem("c")
    s_xn = ph.sem("xn")
    s_stats = ph.sem("stats")
    s_red = ph.sem("red")
    s_tp = ph.sem("tp")
    s_ev = ph.sem("ev")
    s_keep = ph.sem("keep")
    s_col0 = ph.sem("col0")
    s_mix = ph.sem("mix")
    s_mm = ph.sem("mm")
    s_pev = ph.sem("pev")
    s_ksq = ph.sem("ksq")
    s_tb = ph.sem("tb")
    s_add = ph.sem("add")
    s_st0 = ph.sem("st0")
    s_st1 = ph.sem("st1")
    s_k = ph.sem("k")
    s_act = ph.sem("act")

    with nc.Block() as block:

        @block.sync
        def _(sync):
            sync.dma_start(out=ident[:], in_=ident_ap[:]).then_inc(s_c, 16)
            sync.dma_start(out=mur[:], in_=mur_ap.rearrange(
                "(c p) -> p c", p=128),
                allow_slow_non_contiguous=True).then_inc(s_c, 16)
            sync.dma_start(out=muk[:], in_=muk_ap.rearrange(
                "(c p) -> p c", p=128),
                allow_slow_non_contiguous=True).then_inc(s_c, 16)
            for q in range(NQ):
                if q > 0:
                    sync.wait_ge(s_add, 8 * q)
                for h in range(2):
                    r0 = q * QT + h * 512
                    sync.dma_start(
                        out=x3b[:, h * 4:(h + 1) * 4, :],
                        in_=x3_ap[r0:r0 + 512, :].rearrange(
                            "(j p) f -> p j f", p=128),
                    ).then_inc(s_x3, 16)
                for j in range(JQ):
                    gj = q * JQ + j
                    sync.wait_ge(s_add, gj + 1)
                    t0 = q * QT + j * 128
                    sync.dma_start(
                        out=out_ap[t0:t0 + 128, :], in_=outb[:, j % 2, :],
                    ).then_inc(s_st0 if gj % 2 == 0 else s_st1, 16)
            sync.wait_ge(s_st0, 16 * (NT // 2))
            sync.wait_ge(s_st1, 16 * (NT // 2))

        @block.gpsimd
        def _(gpsimd):
            gpsimd.dma_start(out=wr[:], in_=wrT_ap.rearrange(
                "(c p) f -> p c f", p=128)).then_inc(s_w, 16)
            gpsimd.dma_start(out=wk[:], in_=wkT_ap.rearrange(
                "(c p) f -> p c f", p=128)).then_inc(s_w, 16)
            gpsimd.dma_start(out=wv[:], in_=wvT_ap.rearrange(
                "(c p) f -> p c f", p=128)).then_inc(s_w, 16)
            gpsimd.memset(cbuf[:, 0:1], 1e-5).then_inc(s_k, 1)
            gpsimd.memset(xnT[:, :, 0:1], 0.0).then_inc(s_col0, 1)
            for q in range(NQ - 1):
                gpsimd.wait_ge(s_ev, 8 * (q + 1))
                gpsimd.tensor_copy(kcol[:], xnT[:, :, QT:QT + 1]).then_inc(
                    s_keep, 1)
                gpsimd.wait_ge(s_mix, 2 * (q + 1))
                gpsimd.tensor_copy(xnT[:, :, 0:1], kcol[:]).then_inc(
                    s_col0, 1)

        @block.tensor
        def _(tensor):
            tensor.wait_ge(s_c, 48)
            for q in range(NQ):
                for j in range(JQ):
                    gj = q * JQ + j
                    tensor.wait_ge(s_xn, gj + 1)
                    if gj > 0:
                        tensor.wait_ge(s_ev, gj)
                    for cc in range(NCC):
                        inst = tensor.transpose(
                            TP[:, cc, :],
                            xnbf[:, j, cc * 128:(cc + 1) * 128],
                            ident[:],
                        )
                    inst.then_inc(s_tp, 1)
                wq = [48, 48, 48]
                for seg, (wt, src) in enumerate(
                        [(wr, bufA), (wk, bufA), (wv, kactT)]):
                    tensor.wait_ge(s_w, wq[seg])
                    if seg < 2:
                        tensor.wait_ge(s_mix, 2 * q + seg + 1)
                    else:
                        tensor.wait_ge(s_ksq, 16 * (q + 1))
                    for d2c in range(NCC):
                        for tc in range(TC):
                            g = (q * 3 + seg) * 16 + d2c * TC + tc
                            M = M0 if g % 2 == 0 else M1
                            if g >= 2:
                                tensor.wait_ge(s_pev, g - 1)
                            for cc in range(NCC):
                                inst = tensor.matmul(
                                    M[:],
                                    wt[:, cc, d2c * 128:(d2c + 1) * 128],
                                    src[:, cc, tc * 512:(tc + 1) * 512],
                                    start=(cc == 0), stop=(cc == NCC - 1),
                                )
                            inst.then_inc(s_mm, 1)
                for j in range(JQ):
                    gj = q * JQ + j
                    tensor.wait_ge(s_pev, (q * 3 + 3) * 16)
                    if gj > 0:
                        tensor.wait_ge(s_add, gj)
                    for d2c in range(NCC):
                        inst = tensor.transpose(
                            TB[:, d2c, :],
                            bufA[:, d2c, j * 128:(j + 1) * 128],
                            ident[:],
                        )
                    inst.then_inc(s_tb, 1)

        @block.scalar
        def _(scalar):
            scalar.wait_ge(s_k, 1)
            actn = [0]

            def fence(inst):
                actn[0] += 1
                inst.then_inc(s_act, 1)
                scalar.wait_ge(s_act, actn[0])

            for q in range(NQ):
                for j in range(JQ):
                    gj = q * JQ + j
                    scalar.wait_ge(s_red, gj + 1)
                    fence(scalar.mul(negm[:, j:j + 1], sums[:, j:j + 1],
                                     -1.0 / D))
                    fence(scalar.activation(sqdump[:], x3b[:, j, :], AF.Square,
                                            bias=negm[:, j:j + 1],
                                            accum_out=ssq[:, j:j + 1]))
                    fence(scalar.mul(rstd[:, j:j + 1], ssq[:, j:j + 1],
                                     1.0 / D))
                    fence(scalar.add(rstd[:, j:j + 1], rstd[:, j:j + 1], 1e-5))
                    fence(scalar.activation(rstd[:, j:j + 1], rstd[:, j:j + 1],
                                            AF.Ln))
                    scalar.activation(rstd[:, j:j + 1], rstd[:, j:j + 1],
                                      AF.Exp, scale=-0.5).then_inc(s_stats, 1)
                for seg in range(2):
                    for d2c in range(NCC):
                        for tc in range(TC):
                            g = (q * 3 + seg) * 16 + d2c * TC + tc
                            M = M0 if g % 2 == 0 else M1
                            scalar.wait_ge(s_mm, g + 1)
                            if seg == 0:
                                scalar.activation(
                                    rT[:, d2c, tc * 512:(tc + 1) * 512],
                                    M[:], AF.Sigmoid).then_inc(s_pev, 1)
                            else:
                                kk = d2c * TC + tc
                                if kk >= 2:
                                    scalar.wait_ge(s_ksq, 16 * q + kk - 1)
                                scalar.activation(
                                    krelu[:, kk % 2, :], M[:], AF.Relu,
                                ).then_inc(s_pev, 1)

        @block.vector
        def _(vector):
            for q in range(NQ):
                for j in range(JQ):
                    gj = q * JQ + j
                    vector.wait_ge(s_x3, 32 * (q + 1))
                    vector.reduce_sum(sums[:, j:j + 1], x3b[:, j, :],
                                      axis=AX.X).then_inc(s_red, 1)
                for j in range(JQ):
                    gj = q * JQ + j
                    vector.wait_ge(s_stats, gj + 1)
                    vector.tensor_scalar(
                        xnbf[:, j, :], x3b[:, j, :], negm[:, j:j + 1],
                        rstd[:, j:j + 1], AluOpType.add, AluOpType.mult,
                    ).then_inc(s_xn, 1)
                if q > 0:
                    vector.wait_ge(s_keep, q)
                for j in range(JQ):
                    gj = q * JQ + j
                    vector.wait_ge(s_tp, gj + 1)
                    vector.tensor_copy(
                        xnT[:, :, 1 + j * 128:1 + (j + 1) * 128], TP[:],
                    ).then_inc(s_ev, 1)
                vector.wait_ge(s_c, 48)
                vector.wait_ge(s_col0, q + 1)
                if q > 0:
                    vector.wait_ge(s_tb, 8 * q)
                for cc in range(NCC):
                    vector.tensor_tensor(dbuf[:, cc, :], xnT[:, cc, 1:QT + 1],
                                         xnT[:, cc, 0:QT], AluOpType.subtract)
                    vector.tensor_scalar(tbuf[:], dbuf[:, cc, :],
                                         mur[:, cc:cc + 1], None,
                                         AluOpType.mult)
                    inst = vector.tensor_tensor(bufA[:, cc, :],
                                                xnT[:, cc, 0:QT],
                                                tbuf[:], AluOpType.add)
                inst.then_inc(s_mix, 1)
                vector.wait_ge(s_mm, (q * 3 + 1) * 16)
                for cc in range(NCC):
                    vector.tensor_scalar(tbuf[:], dbuf[:, cc, :],
                                         muk[:, cc:cc + 1], None,
                                         AluOpType.mult)
                    inst = vector.tensor_tensor(bufA[:, cc, :],
                                                xnT[:, cc, 0:QT],
                                                tbuf[:], AluOpType.add)
                inst.then_inc(s_mix, 1)
                for d2c in range(NCC):
                    for tc in range(TC):
                        kk = d2c * TC + tc
                        vector.wait_ge(s_pev, (q * 3 + 1) * 16 + kk + 1)
                        vector.tensor_tensor(
                            kactT[:, d2c, tc * 512:(tc + 1) * 512],
                            krelu[:, kk % 2, :], krelu[:, kk % 2, :],
                            AluOpType.mult).then_inc(s_ksq, 1)
                for d2c in range(NCC):
                    for tc in range(TC):
                        g = (q * 3 + 2) * 16 + d2c * TC + tc
                        M = M0 if g % 2 == 0 else M1
                        vector.wait_ge(s_mm, g + 1)
                        vector.tensor_tensor(
                            bufA[:, d2c, tc * 512:(tc + 1) * 512],
                            rT[:, d2c, tc * 512:(tc + 1) * 512],
                            M[:], AluOpType.mult).then_inc(s_pev, 1)
                for j in range(JQ):
                    gj = q * JQ + j
                    vector.wait_ge(s_tb, gj + 1)
                    if gj >= 2:
                        vector.wait_ge(s_st0 if gj % 2 == 0 else s_st1,
                                       16 * (gj // 2))
                    vector.tensor_tensor(
                        outb[:, j % 2, :], x3b[:, j, :],
                        TB[:].rearrange("p a b -> p (a b)"),
                        AluOpType.add).then_inc(s_add, 1)

    ph.close()
    return ph



def build_phase1(nc, x1_ap, x2_ap, gateT_ap, wg_ap, wu_ap, wd_ap, swg_ap,
                 swu_ap, swd_ap, ident_ap, tag="p1", n_experts=10,
                 debug=None, sem_stack=None, gcbuf=None):
    ph = Phase(nc, tag, sem_stack)
    QT = 1024
    NQ = S // QT
    JQ = QT // 128
    NI = 11
    NE = n_experts

    x1b = ph.sbuf("x1b", [128, JQ, D], F32)
    xnbf = ph.sbuf("xnbf", [128, JQ, D], BF16)
    xnT = ph.sbuf("xnT", [128, NCC, QT], BF16)
    actT = ph.sbuf("actT", [128, NI, QT], BF16)
    bg = ph.sbuf("bg", [128, NCC, 1408], BF16)
    bu = ph.sbuf("bu", [128, NCC, 1408], BF16)
    bd = ph.sbuf("bd", [128, NI, D], BF16)
    gt = ph.sbuf("gt", [128, NCC, 8], BF16)
    sums = ph.sbuf("sums", [128, JQ], F32)
    negm = ph.sbuf("negm", [128, JQ], F32)
    ssq = ph.sbuf("ssq", [128, JQ], F32)
    rstd = ph.sbuf("rstd", [128, JQ], F32)
    sqdump = ph.sbuf("sqdump", [128, D], F32)
    ident = ph.sbuf("ident", [128, 128], BF16)
    cbuf = gcbuf if gcbuf is not None else ph.sbuf("cbuf", [128, 4], F32)
    ensure_const(nc, 1e-5, cbuf[:, 0:1])
    sgl = ph.sbuf("sgl", [128, 512], F32)
    tmp2 = ph.sbuf("tmp2", [128, 2, 512], F32)
    wbuf = ph.sbuf("wbuf", [128, JQ, 8], F32)
    gsc = ph.sbuf("gsc", [128, 8], F32)
    gcol = ph.sbuf("gcol", [128, 6], F32)
    g1 = ph.sbuf("g1", [128, 8], F32)
    g2 = ph.sbuf("g2", [128, 8], F32)
    g3 = ph.sbuf("g3", [128, 8], F32)
    gsg = ph.sbuf("gsg", [128, 8], F32)

    TP = ph.psum("TP", [128, 8, 128], BF16)
    GA = ph.psum("GA", [128, 512], F32)
    GB = ph.psum("GB", [128, 512], F32)
    C0 = ph.psum("C0", [128, 512], F32)
    C1 = ph.psum("C1", [128, 512], F32)
    LG = ph.psum("LG", [128, 8], F32)

    s_x1 = ph.sem("x1")
    s_bg = ph.sem("bg")
    s_bu = ph.sem("bu")
    s_bd = ph.sem("bd")
    s_c = ph.sem("c")
    s_gt = ph.sem("gt")
    s_k = ph.sem("k")
    s_red = ph.sem("red")
    s_stats = ph.sem("stats")
    s_xn = ph.sem("xn")
    s_tp = ph.sem("tp")
    s_ev = ph.sem("ev")
    s_act = ph.sem("act")
    s_gmm = ph.sem("gmm")
    s_gpre = ph.sem("gpre")
    s_gexp = ph.sem("gexp")
    s_gv = ph.sem("gv")
    s_gd1 = ph.sem("gd1")
    s_gsg1 = ph.sem("gsg1")
    s_gd2 = ph.sem("gd2")
    s_gsg2 = ph.sem("gsg2")
    s_gate = ph.sem("gate")
    s_mmgu = ph.sem("mmgu")
    s_silu = ph.sem("silu")
    s_fuse = ph.sem("fuse")
    s_mmd = ph.sem("mmd")
    s_ev2 = ph.sem("ev2")
    s_acc = ph.sem("acc")
    s_st = ph.sem("st")

    def esrc(e):
        if e < 8:
            return (wg_ap[e], wu_ap[e], wd_ap[e])
        h = e - 8
        return (swg_ap[:, h * 1408:(h + 1) * 1408],
                swu_ap[:, h * 1408:(h + 1) * 1408],
                swd_ap[h * 1408:(h + 1) * 1408, :])

    GU_PER_EQ = NI * 2 * 2
    D_PER_EQ = JQ * 2

    with nc.Block() as block:

        @block.sync
        def _(sync):
            sync.dma_start(out=ident[:], in_=ident_ap[:]).then_inc(s_c, 16)
            pass
            for q in range(NQ):
                if q > 0:
                    sync.wait_ge(s_st, 32 * q)
                for h in range(2):
                    r0 = q * QT + h * 512
                    sync.dma_start(
                        out=x1b[:, h * 4:(h + 1) * 4, :],
                        in_=x1_ap[r0:r0 + 512, :].rearrange(
                            "(j p) f -> p j f", p=128),
                    ).then_inc(s_x1, 16)
                if debug is not None and q == 0:
                    sync.wait_ge(s_ev, JQ)
                    sync.dma_start(out=debug["xnT"][:],
                                   in_=xnT[:]).then_inc(s_st, 16)
                    sync.wait_ge(s_gate, JQ)
                    sync.dma_start(out=debug["wbuf"][:],
                                   in_=wbuf[:]).then_inc(s_st, 16)
                    for nm, buf in [("gsc", gsc), ("g1", g1), ("g2", g2),
                                    ("g3", g3), ("gsg", gsg), ("gcol", gcol)]:
                        sync.dma_start(out=debug[nm][:],
                                       in_=buf[:]).then_inc(s_st, 16)
                    sync.wait_ge(s_fuse, NI * 2)
                    sync.dma_start(out=debug["actT"][:],
                                   in_=actT[:]).then_inc(s_st, 16)
                sync.wait_ge(s_acc, NE * D_PER_EQ * (q + 1))
                for h in range(2):
                    r0 = q * QT + h * 512
                    sync.dma_start(
                        out=x2_ap[r0:r0 + 512, :].rearrange(
                            "(j p) f -> p j f", p=128),
                        in_=x1b[:, h * 4:(h + 1) * 4, :],
                    ).then_inc(s_st, 16)
            sync.wait_ge(s_st, 16 * 2 * NQ + (144 if debug is not None else 0))

        @block.gpsimd
        def _(gpsimd):
            gpsimd.memset(cbuf[:, 0:1], 1e-5)
            gpsimd.memset(cbuf[:, 1:2], 0.0)
            gpsimd.memset(cbuf[:, 2:3], 1.0)
            gpsimd.memset(cbuf[:, 3:4], 1e-20).then_inc(s_k, 1)
            gpsimd.dma_start(out=gt[:], in_=gateT_ap.rearrange(
                "(c p) e -> p c e", p=128)).then_inc(s_gt, 16)
            n = 0
            for q in range(NQ):
                for e in range(NE):
                    sg, su, sd = esrc(e)
                    n += 1
                    if n > 1:
                        gpsimd.wait_ge(s_mmgu, GU_PER_EQ * (n - 1))
                    gpsimd.dma_start(out=bg[:], in_=sg.rearrange(
                        "(c p) f -> p c f", p=128)).then_inc(s_bg, 16)
                    gpsimd.dma_start(out=bu[:], in_=su.rearrange(
                        "(c p) f -> p c f", p=128)).then_inc(s_bu, 16)
                    if n > 1:
                        gpsimd.wait_ge(s_mmd, D_PER_EQ * (n - 1))
                    gpsimd.dma_start(out=bd[:], in_=sd.rearrange(
                        "(i p) f -> p i f", p=128)).then_inc(s_bd, 16)

        @block.tensor
        def _(tensor):
            tensor.wait_ge(s_c, 16)
            tensor.wait_ge(s_gt, 16)
            gu_g = 0
            d_g = 0
            for q in range(NQ):
                for j in range(JQ):
                    gj = q * JQ + j
                    tensor.wait_ge(s_xn, gj + 1)
                    if gj > 0:
                        tensor.wait_ge(s_ev, gj)
                    for cc in range(NCC):
                        inst = tensor.transpose(
                            TP[:, cc, :],
                            xnbf[:, j, cc * 128:(cc + 1) * 128],
                            ident[:])
                    inst.then_inc(s_tp, 1)
                for j in range(JQ):
                    gj = q * JQ + j
                    tensor.wait_ge(s_ev, gj + 1)
                    if gj > 0:
                        tensor.wait_ge(s_gate, gj)
                    for cc in range(NCC):
                        inst = tensor.matmul(
                            LG[:], xnT[:, cc, j * 128:(j + 1) * 128],
                            gt[:, cc, :],
                            start=(cc == 0), stop=(cc == NCC - 1))
                    inst.then_inc(s_gmm, 1)
                tensor.wait_ge(s_ev, JQ * (q + 1))
                for e in range(NE):
                    n = q * NE + e + 1
                    tensor.wait_ge(s_bg, 16 * n)
                    tensor.wait_ge(s_bu, 16 * n)
                    for ic in range(NI):
                        for tc2 in range(2):
                            for gu, (wt, MM) in enumerate([(bg, GA),
                                                           (bu, GB)]):
                                gu_g += 1
                                if gu == 0 and gu_g > 2:
                                    tensor.wait_ge(s_silu, (gu_g - 1) // 2)
                                if gu == 1 and gu_g > 2:
                                    tensor.wait_ge(s_fuse, (gu_g - 2) // 2)
                                for cc in range(NCC):
                                    inst = tensor.matmul(
                                        MM[:],
                                        wt[:, cc,
                                           ic * 128:(ic + 1) * 128],
                                        xnT[:, cc,
                                            tc2 * 512:(tc2 + 1) * 512],
                                        start=(cc == 0),
                                        stop=(cc == NCC - 1))
                                inst.then_inc(s_mmgu, 1)
                    tensor.wait_ge(s_bd, 16 * n)
                    tensor.wait_ge(s_fuse, NI * 2 * n)
                    for j in range(JQ):
                        for h2 in range(2):
                            d_g += 1
                            M = C0 if d_g % 2 == 1 else C1
                            if d_g > 2:
                                tensor.wait_ge(s_ev2, d_g - 2)
                            for ic in range(NI):
                                inst = tensor.matmul(
                                    M[:],
                                    actT[:, ic, j * 128:(j + 1) * 128],
                                    bd[:, ic, h2 * 512:(h2 + 1) * 512],
                                    start=(ic == 0), stop=(ic == NI - 1))
                            inst.then_inc(s_mmd, 1)

        @block.scalar
        def _(scalar):
            scalar.wait_ge(s_k, 1)
            actn = [0]

            def fence(inst):
                actn[0] += 1
                inst.then_inc(s_act, 1)
                scalar.wait_ge(s_act, actn[0])

            gu_g = 0
            d_g = 0
            for q in range(NQ):
                for j in range(JQ):
                    gj = q * JQ + j
                    scalar.wait_ge(s_red, gj + 1)
                    fence(scalar.mul(negm[:, j:j + 1], sums[:, j:j + 1],
                                     -1.0 / D))
                    fence(scalar.activation(sqdump[:], x1b[:, j, :],
                                            AF.Square, bias=negm[:, j:j + 1],
                                            accum_out=ssq[:, j:j + 1]))
                    fence(scalar.mul(rstd[:, j:j + 1], ssq[:, j:j + 1],
                                     1.0 / D))
                    fence(scalar.add(rstd[:, j:j + 1], rstd[:, j:j + 1],
                                     1e-5))
                    fence(scalar.activation(rstd[:, j:j + 1],
                                            rstd[:, j:j + 1], AF.Ln))
                    scalar.activation(rstd[:, j:j + 1], rstd[:, j:j + 1],
                                      AF.Exp, scale=-0.5).then_inc(s_stats, 1)
                for j in range(JQ):
                    gj = q * JQ + j
                    scalar.wait_ge(s_gpre, gj + 1)
                    scalar.activation(gsc[:], LG[:], AF.Exp,
                                      bias=gcol[:, 1:2]).then_inc(s_gexp, 1)
                    scalar.wait_ge(s_gd1, gj + 1)
                    scalar.activation(gsg[:], g1[:],
                                      AF.Sign).then_inc(s_gsg1, 1)
                    scalar.wait_ge(s_gd2, gj + 1)
                    scalar.activation(gsg[:], g3[:],
                                      AF.Sign).then_inc(s_gsg2, 1)
                for e in range(NE):
                    for ic in range(NI):
                        for tc2 in range(2):
                            gu_g += 2
                            scalar.wait_ge(s_mmgu, gu_g - 1)
                            if gu_g > 2:
                                scalar.wait_ge(s_fuse, gu_g // 2 - 1)
                            scalar.activation(sgl[:], GA[:],
                                              AF.Silu).then_inc(s_silu, 1)
                    for j in range(JQ):
                        for h2 in range(2):
                            d_g += 1
                            M = C0 if d_g % 2 == 1 else C1
                            scalar.wait_ge(s_mmd, d_g)
                            if d_g > 2:
                                scalar.wait_ge(s_acc, d_g - 2)
                            if e < 8:
                                scalar.activation(
                                    tmp2[:, d_g % 2, :], M[:], AF.Identity,
                                    scale=wbuf[:, j, e:e + 1],
                                ).then_inc(s_ev2, 1)
                            else:
                                scalar.activation(
                                    tmp2[:, d_g % 2, :], M[:], AF.Identity,
                                ).then_inc(s_ev2, 1)

        @block.vector
        def _(vector):
            gu_g = 0
            d_g = 0
            gvn = [0]
            gvn_ms = [0]

            def gvn_sub(n):
                return n - gvn_ms[0]

            for q in range(NQ):
                for j in range(JQ):
                    vector.wait_ge(s_x1, 32 * (q + 1))
                    vector.reduce_sum(sums[:, j:j + 1], x1b[:, j, :],
                                      axis=AX.X).then_inc(s_red, 1)
                for j in range(JQ):
                    gj = q * JQ + j
                    vector.wait_ge(s_stats, gj + 1)
                    vector.tensor_scalar(
                        xnbf[:, j, :], x1b[:, j, :], negm[:, j:j + 1],
                        rstd[:, j:j + 1], AluOpType.add, AluOpType.mult,
                    ).then_inc(s_xn, 1)
                for j in range(JQ):
                    gj = q * JQ + j
                    vector.wait_ge(s_tp, gj + 1)
                    vector.tensor_copy(
                        xnT[:, :, j * 128:(j + 1) * 128], TP[:],
                    ).then_inc(s_ev, 1)
                vector.wait_ge(s_k, 1)
                for j in range(JQ):
                    gj = q * JQ + j

                    def vf(inst, ms=None):
                        gvn[0] += 1
                        if ms is not None:
                            gvn_ms[0] += 1
                            inst.then_inc(ms, 1)
                        else:
                            inst.then_inc(s_gv, 1)

                    def vw():
                        vector.wait_ge(s_gv, gvn_sub(gvn[0]))

                    vector.wait_ge(s_gmm, gj + 1)
                    if gj > 0:
                        vector.wait_ge(s_gate, gj)
                        vector.wait_ge(s_gpre, gj)
                    vf(vector.reduce_max(gcol[:, 0:1], LG[:], axis=AX.X))
                    vw()
                    vf(vector.tensor_scalar(gcol[:, 1:2], gcol[:, 0:1], -1.0,
                                            None, AluOpType.mult), s_gpre)
                    vector.wait_ge(s_gexp, gj + 1)
                    vf(vector.reduce_sum(gcol[:, 2:3], gsc[:], axis=AX.X))
                    vw()
                    vf(vector.reciprocal(gcol[:, 2:3], gcol[:, 2:3]))
                    vw()
                    vf(vector.tensor_scalar(gsc[:], gsc[:], gcol[:, 2:3],
                                            None, AluOpType.mult))
                    vw()
                    vf(vector.reduce_max(gcol[:, 3:4], gsc[:], axis=AX.X))
                    vw()
                    vf(vector.tensor_scalar(g1[:], gsc[:], cbuf[:, 1:2],
                                            gcol[:, 3:4], AluOpType.mult,
                                            AluOpType.add))
                    vw()
                    vf(vector.tensor_tensor(g1[:], gsc[:], g1[:],
                                            AluOpType.subtract), s_gd1)
                    vector.wait_ge(s_gsg1, gj + 1)
                    vf(vector.tensor_scalar(g1[:], gsg[:], cbuf[:, 2:3],
                                            None, AluOpType.add))
                    vw()
                    vf(vector.tensor_scalar(g2[:], g1[:], 1e30, None,
                                            AluOpType.mult))
                    vw()
                    vf(vector.tensor_tensor(g2[:], gsc[:], g2[:],
                                            AluOpType.subtract))
                    vw()
                    vf(vector.reduce_max(gcol[:, 4:5], g2[:], axis=AX.X))
                    vw()
                    vf(vector.tensor_scalar(g3[:], gsc[:], cbuf[:, 1:2],
                                            gcol[:, 4:5], AluOpType.mult,
                                            AluOpType.add))
                    vw()
                    vf(vector.tensor_tensor(g3[:], g2[:], g3[:],
                                            AluOpType.subtract), s_gd2)
                    vector.wait_ge(s_gsg2, gj + 1)
                    vf(vector.tensor_scalar(g3[:], gsg[:], cbuf[:, 2:3],
                                            None, AluOpType.add))
                    vw()
                    vf(vector.tensor_tensor(g3[:], g1[:], g3[:],
                                            AluOpType.add))
                    vw()
                    vf(vector.tensor_tensor(g3[:], gsc[:], g3[:],
                                            AluOpType.mult))
                    vw()
                    vf(vector.tensor_tensor(gcol[:, 5:6], gcol[:, 3:4],
                                            gcol[:, 4:5], AluOpType.add))
                    vw()
                    vf(vector.tensor_scalar(gcol[:, 5:6], gcol[:, 5:6],
                                            cbuf[:, 3:4], None,
                                            AluOpType.add))
                    vw()
                    vf(vector.reciprocal(gcol[:, 5:6], gcol[:, 5:6]))
                    vw()
                    vf(vector.tensor_scalar(wbuf[:, j, :], g3[:],
                                            gcol[:, 5:6], None,
                                            AluOpType.mult), s_gate)
                for e in range(NE):
                    for ic in range(NI):
                        for tc2 in range(2):
                            gu_g += 2
                            vector.wait_ge(s_silu, gu_g // 2)
                            vector.wait_ge(s_mmgu, gu_g)
                            vector.tensor_tensor(
                                actT[:, ic, tc2 * 512:(tc2 + 1) * 512],
                                sgl[:], GB[:],
                                AluOpType.mult).then_inc(s_fuse, 1)
                    for j in range(JQ):
                        for h2 in range(2):
                            d_g += 1
                            vector.wait_ge(s_ev2, d_g)
                            vector.tensor_tensor(
                                x1b[:, j, h2 * 512:(h2 + 1) * 512],
                                x1b[:, j, h2 * 512:(h2 + 1) * 512],
                                tmp2[:, d_g % 2, :],
                                AluOpType.add).then_inc(s_acc, 1)

    ph.close()
    return ph




def build_attn(nc, layer, x_ap, out_ap, wq_ap, wk_ap, wv_ap, wo_ap,
               bqkv_ap, bo_ap, ident_ap, ones164_ap, ones128_ap,
               ones1128_ap, scr_q, scr_k, scr_v, scr_o, scr_c,
               pe_ap=None, h_ap=None, gb_ap=None,
               ropeC_ap=None, ropeS_ap=None, swap_ap=None, tag="pa",
               sem_stack=None, gcbuf=None):
    ph = Phase(nc, tag, sem_stack)
    QT = 1024
    NQ = S // QT
    JQ = QT // 128
    NP = 8

    AR1 = ph.sbuf("AR1", [128, 57408], BF16)
    hT_ = AR1[:, 0:32768].rearrange("p (c t) -> p c t", c=NCC)
    xnbf_ = AR1[:, 32768:40960].rearrange("p (j d) -> p j d", j=JQ)
    qT_ = AR1[:, 40960:45056]
    kT_ = AR1[:, 45056:49152]
    raw_ = AR1[:, 49152:53248]
    vag_ = AR1[:, 53248:57408].rearrange("p (k v) -> p k v", k=NT)
    cq_ = AR1[:, 0:8192].rearrange("p (s t) -> p s t", s=2)
    ck_ = AR1[:, 8192:16384].rearrange("p (s t) -> p s t", s=2)
    cv_ = AR1[:, 16384:24704].rearrange("p (s k v) -> p s k v", s=2, k=NT)
    onT_ = AR1[:, 24704:28800]
    pbuf_ = AR1[:, 28800:38016].rearrange("p (s q) -> p s q", s=18)
    wob_ = AR1[:, 38016:46208].rearrange("p (c f) -> p c f", c=NCC)
    olh_ = AR1[:, 46208:48256].rearrange("p (s c f) -> p s c f", s=2, c=NCC)

    xb = ph.sbuf("xb", [128, JQ, D], F32)
    peb = ph.sbuf("peb", [128, 2, D], F32) if layer == 0 else None
    sums = ph.sbuf("sums", [128, JQ], F32)
    negm = ph.sbuf("negm", [128, JQ], F32)
    ssq = ph.sbuf("ssq", [128, JQ], F32)
    rstd = ph.sbuf("rstd", [128, JQ], F32)
    sqdump = ph.sbuf("sqdump", [128, D], F32)
    ident = ph.sbuf("ident", [128, 128], BF16)
    ones164 = ph.sbuf("ones164", [1, 64], BF16)
    ones128 = ph.sbuf("ones128", [128, 1], BF16)
    ones1128 = ph.sbuf("ones1128", [1, 128], F32)
    cbuf = gcbuf if gcbuf is not None else ph.sbuf("cbuf", [128, 4], F32)
    ensure_const(nc, 1e-5, cbuf[:, 0:1])
    wqb = ph.sbuf("wqb", [128, NCC, 128], BF16)
    wkb = ph.sbuf("wkb", [128, NCC, 128], BF16)
    wvb = ph.sbuf("wvb", [128, NCC, 128], BF16)
    bqkv = ph.sbuf("bqkv", [128, NCC, 3], F32)
    sq = ph.sbuf("sq", [128, 512], BF16)
    nmx = ph.sbuf("nmx", [1, 8], F32)
    ncl = ph.sbuf("ncl", [1, 4], F32)
    negc = ph.sbuf("negc", [128, 1], F32)
    cnegc = ph.sbuf("cnegc", [128, 2], F32)
    if layer == 1:
        ropeC = ph.sbuf("ropeC", [128, S], BF16)
        ropeS = ph.sbuf("ropeS", [128, S], BF16)
        swap = ph.sbuf("swap", [128, 128], BF16)
    rt1 = ph.sbuf("rt1", [128, 512], BF16)
    aev = ph.sbuf("aev", [128, 512], BF16)
    recd = ph.sbuf("recd", [1, 512], BF16)
    bo = ph.sbuf("bo", [128, D], F32)
    gb = ph.sbuf("gb", [128, 2, D], F32) if layer == 0 else None
    hb = ph.sbuf("hb", [128, 2, D], F32)
    xr2 = ph.sbuf("xr2", [128, 2, D], F32) if layer == 0 else None
    outb = ph.sbuf("outb", [128, 2, D], F32)

    B01 = ph.psum("B01", [128, 1024], F32)
    B23 = ph.psum("B23", [128, 1024], F32)
    B4 = ph.psum("B4", [128, 512], F32)
    B5 = ph.psum("B5", [128, 512], F32)
    B6 = ph.psum("B6", [128, 512], F32)
    B7 = ph.psum("B7", [128, 8, 128], BF16)

    s_x = ph.sem("x")
    s_pe = ph.sem("pe") if layer == 0 else None
    s_red = ph.sem("red")
    s_stats = ph.sem("stats")
    s_xn = ph.sem("xn")
    s_tp = ph.sem("tp")
    s_ev = ph.sem("ev")
    s_act = ph.sem("act")
    s_vf = ph.sem("vf")
    s_hst = ph.sem("hst") if layer == 0 else None
    s_c = ph.sem("c")
    s_wp = ph.sem("wp")
    s_wo = ph.sem("wo")
    s_Bmm = ph.sem("Bmm")
    s_Bev = ph.sem("Bev")
    s_rot = ph.sem("rot")
    s_rope = ph.sem("rope")
    s_sqn = ph.sem("sqn")
    s_nmm = ph.sem("nmm")
    s_nred = ph.sem("nred")
    s_nprod = ph.sem("nprod")
    s_nsq = ph.sem("nsq")
    s_ngmm = ph.sem("ngmm")
    s_ngc = ph.sem("ngc")
    s_vtp = ph.sem("vtp")
    s_vev = ph.sem("vev")
    s_Bst = ph.sem("Bst")
    s_cl0 = ph.sem("cl0")
    s_cl1 = ph.sem("cl1")
    s_sc = ph.sem("sc")
    s_px = ph.sem("px")
    s_avm = ph.sem("avm")
    s_cev = ph.sem("cev")
    s_nrm = ph.sem("nrm")
    s_bc = ph.sem("bc")
    s_on = ph.sem("on")
    s_ost = ph.sem("ost")
    s_dl0 = ph.sem("dl0")
    s_dl1 = ph.sem("dl1")
    s_dmm = ph.sem("dmm")
    s_hr0 = ph.sem("hr0")
    s_hr1 = ph.sem("hr1")
    s_xr0 = ph.sem("xr0")
    s_xr1 = ph.sem("xr1")
    s_ep = ph.sem("ep")
    s_st0 = ph.sem("st0")
    s_st1 = ph.sem("st1")
    s_k = ph.sem("k")
    s_tab = ph.sem("tab")

    n_const = 6 + (1 if layer == 0 else 0)
    L1 = (layer == 1)

    with nc.Block() as block:

        @block.sync
        def _(sync):
            sync.dma_start(out=ident[:], in_=ident_ap[:]).then_inc(s_c, 16)
            sync.dma_start(out=ones164[:], in_=ones164_ap[:]).then_inc(s_c, 16)
            sync.dma_start(out=ones128[:], in_=ones128_ap[:]).then_inc(s_c, 16)
            sync.dma_start(out=ones1128[:], in_=ones1128_ap[:]).then_inc(s_c, 16)
            sync.dma_start(out=bqkv[:], in_=bqkv_ap[:]).then_inc(s_c, 16)
            sync.dma_start(out=bo[:], in_=bo_ap[:]).then_inc(s_c, 16)
            if layer == 0:
                sync.dma_start(out=gb[:], in_=gb_ap[:]).then_inc(s_c, 16)
            for q in range(NQ):
                for h2 in range(2):
                    r0 = q * QT + h2 * 512
                    sync.dma_start(
                        out=xb[:, h2 * 4:(h2 + 1) * 4, :],
                        in_=x_ap[r0:r0 + 512, :].rearrange(
                            "(j p) f -> p j f", p=128),
                    ).then_inc(s_x, 16)
                if layer == 0:
                    for j in range(JQ):
                        gj = q * JQ + j
                        sl = gj % 2
                        if gj >= 2:
                            sync.wait_ge(s_xn, gj - 1)
                        sync.dma_start(
                            out=peb[:, sl, :],
                            in_=pe_ap[gj * 128:(gj + 1) * 128, :],
                        ).then_inc(s_pe, 16)
                    sync.wait_ge(s_xn, JQ * (q + 1))
                    for h2 in range(2):
                        r0 = q * QT + h2 * 512
                        sync.dma_start(
                            out=h_ap[r0:r0 + 512, :].rearrange(
                                "(j p) f -> p j f", p=128),
                            in_=xb[:, h2 * 4:(h2 + 1) * 4, :],
                        ).then_inc(s_hst, 16)
                else:
                    sync.wait_ge(s_xn, JQ * (q + 1))
                if q < NQ - 1:
                    sync.wait_ge(s_ev, JQ * (q + 1))
                    if layer == 0:
                        sync.wait_ge(s_hst, 32 * (q + 1))
            for p in range(NP):
                if L1:
                    sync.wait_ge(s_rope, 16 * (p + 1))
                else:
                    sync.wait_ge(s_Bev, (p * 3 + 2) * 8)
                sync.dma_start(out=scr_q[:, p, :], in_=qT_).then_inc(
                    s_Bst, 16)
                sync.dma_start(out=scr_k[:, p, :], in_=kT_).then_inc(
                    s_Bst, 16)
                sync.wait_ge(s_vev, 4 * (p + 1))
                sync.dma_start(out=scr_v[:, p, :, :], in_=vag_).then_inc(
                    s_Bst, 16)
                sync.wait_ge(s_ngc, p + 1)
                sync.dma_start(out=scr_c[:, p:p + 1], in_=negc[:],
                               allow_slow_non_contiguous=True).then_inc(
                    s_Bst, 16)
            sync.wait_ge(s_Bst, 16 * 4 * NP)
            for p in range(NP):
                sl = p % 2
                s_cl = s_cl0 if sl == 0 else s_cl1
                if p >= 2:
                    sync.wait_ge(s_ost, 16 * (p - 1))
                sync.dma_start(out=cq_[:, sl, :],
                               in_=scr_q[:, p, :]).then_inc(s_cl, 16)
                sync.dma_start(out=ck_[:, sl, :],
                               in_=scr_k[:, p, :]).then_inc(s_cl, 16)
                sync.dma_start(out=cv_[:, sl, :, :],
                               in_=scr_v[:, p, :, :]).then_inc(s_cl, 16)
                sync.dma_start(out=cnegc[:, sl:sl + 1],
                               in_=scr_c[:, p:p + 1],
                               allow_slow_non_contiguous=True).then_inc(
                    s_cl, 16)
                if p > 0:
                    sync.wait_ge(s_on, 16 * p)
                    sync.dma_start(out=scr_o[:, p - 1, :],
                                   in_=onT_).then_inc(s_ost, 16)
            sync.wait_ge(s_on, 16 * NP)
            sync.dma_start(out=scr_o[:, NP - 1, :], in_=onT_).then_inc(
                s_ost, 16)
            for t in range(NT):
                sl = t % 2
                s_dl = s_dl0 if sl == 0 else s_dl1
                s_hr = s_hr0 if sl == 0 else s_hr1
                if t == 0:
                    sync.wait_ge(s_ost, 16 * NP)
                    if layer == 0:
                        sync.wait_ge(s_hst, 32 * NQ)
                if t >= 2:
                    sync.wait_ge(s_dmm, 2 * (t - 1))
                sync.dma_start(
                    out=olh_[:, sl, :, :],
                    in_=scr_o[:, :, t * 128:(t + 1) * 128],
                ).then_inc(s_dl, 16)
                if t >= 2:
                    sync.wait_ge(s_ep, t - 1)
                src_res = h_ap if layer == 0 else x_ap
                sync.dma_start(
                    out=hb[:, sl, :],
                    in_=src_res[t * 128:(t + 1) * 128, :],
                ).then_inc(s_hr, 16)
                if layer == 0:
                    s_xr = s_xr0 if sl == 0 else s_xr1
                    sync.dma_start(
                        out=xr2[:, sl, :],
                        in_=x_ap[t * 128:(t + 1) * 128, :],
                    ).then_inc(s_xr, 16)
                sync.wait_ge(s_ep, t + 1)
                sync.dma_start(
                    out=out_ap[t * 128:(t + 1) * 128, :],
                    in_=outb[:, t % 2, :],
                ).then_inc(s_st0 if t % 2 == 0 else s_st1, 16)
            sync.wait_ge(s_st0, 16 * (NT // 2))
            sync.wait_ge(s_st1, 16 * (NT // 2))

        @block.gpsimd
        def _(gpsimd):
            gpsimd.memset(cbuf[:, 0:1], 1e-5).then_inc(s_k, 1)
            gpsimd.memset(vag_[:, :, 64:65], 1.0)
            gpsimd.memset(vag_[:, :, 129:130], 1.0).then_inc(s_k, 1)
            if L1:
                gpsimd.dma_start(out=swap[:], in_=swap_ap[:]).then_inc(
                    s_tab, 16)
                gpsimd.dma_start(out=ropeC[:], in_=ropeC_ap[:]).then_inc(
                    s_tab, 16)
                gpsimd.dma_start(out=ropeS[:], in_=ropeS_ap[:]).then_inc(
                    s_tab, 16)
            for p in range(NP):
                if p > 0:
                    gpsimd.wait_ge(s_Bmm, 24 * p)
                for w_ap, wb in [(wq_ap, wqb), (wk_ap, wkb), (wv_ap, wvb)]:
                    gpsimd.dma_start(
                        out=wb[:],
                        in_=w_ap[:, p * 128:(p + 1) * 128].rearrange(
                            "(c p) f -> p c f", p=128),
                    ).then_inc(s_wp, 16)

            gpsimd.wait_ge(s_Bst, 64 * NP)
            gpsimd.dma_start(out=wob_[:, :, :], in_=wo_ap.rearrange(
                "(c p) f -> p c f", p=128)).then_inc(s_wo, 16)

        @block.tensor
        def _(tensor):
            tensor.wait_ge(s_c, 16 * n_const)
            tensor.wait_ge(s_k, 2)
            for q in range(NQ):
                for j in range(JQ):
                    gj = q * JQ + j
                    tensor.wait_ge(s_xn, gj + 1)
                    if gj > 0:
                        tensor.wait_ge(s_ev, gj)
                    for cc in range(NCC):
                        inst = tensor.transpose(
                            B7[:, cc, :],
                            xnbf_[:, j, cc * 128:(cc + 1) * 128],
                            ident[:])
                    inst.then_inc(s_tp, 1)
            PSLOT = [B01[:, 0:512], B01[:, 512:1024], B4[:, :], B5[:, :]]
            for p in range(NP):
                tensor.wait_ge(s_wp, 48 * (p + 1))
                for mi in range(3):
                    wb = [wqb, wkb, wvb][mi]
                    for qc in range(8):
                        g = (p * 3 + mi) * 8 + qc + 1
                        Mv = PSLOT[(g - 1) % 4]
                        tensor.wait_ge(s_ev, min(NT, 4 * (qc + 1)))
                        if g > 4:
                            tensor.wait_ge(s_Bev, g - 4)
                        for cc in range(NCC):
                            inst = tensor.matmul(
                                Mv, wb[:, cc, :],
                                hT_[:, cc, qc * 512:(qc + 1) * 512],
                                start=(cc == 0), stop=(cc == NCC - 1))
                        inst.then_inc(s_Bmm, 1)
                    if mi < 2:
                        if L1:
                            tensor.wait_ge(s_tab, 48)
                            if mi == 0 and p > 0:
                                tensor.wait_ge(s_ngc, p)
                            for qc in range(8):
                                rg = (p * 2 + mi) * 8 + qc + 1
                                tensor.wait_ge(
                                    s_Bev, (p * 3 + mi) * 8 + qc + 1)
                                if rg > 1:
                                    tensor.wait_ge(s_rope, rg - 1)
                                tensor.matmul(
                                    B6[:], swap[:],
                                    raw_[:, qc * 512:(qc + 1) * 512],
                                    start=True, stop=True).then_inc(s_rot, 1)
                        for qc in range(8):
                            n = (p * 2 + mi) * 8 + qc + 1
                            tensor.wait_ge(s_sqn, n)
                            if n > 1:
                                tensor.wait_ge(s_nred, n - 1)
                            tensor.matmul(
                                B23[0:1, 0:512], ones128[:], sq[:],
                                start=True, stop=True).then_inc(s_nmm, 1)
                        if mi == 1:
                            tensor.wait_ge(s_nsq, p + 1)
                            if L1:
                                tensor.wait_ge(s_rope, 16 * (p + 1))
                            elif p > 0:
                                tensor.wait_ge(s_ngc, p)
                            tensor.matmul(
                                B6[:, 0:1], ones1128[:], ncl[0:1, 3:4],
                                start=True, stop=True).then_inc(s_ngmm, 1)
                    else:
                        for tg in range(4):
                            tensor.wait_ge(
                                s_Bev, (p * 3 + 2) * 8 + 2 * tg + 2)
                            tensor.wait_ge(s_vev, 4 * p + tg)
                            for tt in range(8):
                                kc = tg * 8 + tt
                                inst = tensor.transpose(
                                    B7[:, tt, :],
                                    raw_[:, kc * 128:(kc + 1) * 128],
                                    ident[:])
                            inst.then_inc(s_vtp, 1)
            sc_g = 0
            bc_g = 0
            for p in range(NP):
                sl = p % 2
                s_cl = s_cl0 if sl == 0 else s_cl1
                tensor.wait_ge(s_cl, 64 * (p // 2 + 1))
                for qc in range(8):
                    aq = p * 8 + qc

                    def emit_av(kc, exp_cnt):
                        tensor.wait_ge(s_px, exp_cnt)
                        if kc == 0:
                            tensor.wait_ge(s_cev, 2 * aq)
                            tensor.wait_ge(s_nrm, 2 * aq)
                        u1 = exp_cnt - 1
                        tensor.matmul(
                            B4[0:65, :], cv_[:, sl, kc, 0:65],
                            pbuf_[:, (u1 - 1) % 18, :],
                            start=(kc == 0), stop=(kc == NT - 1))
                        tensor.matmul(
                            B5[0:65, :], cv_[:, sl, kc, 65:130],
                            pbuf_[:, u1 % 18, :],
                            start=(kc == 0), stop=(kc == NT - 1),
                        ).then_inc(s_avm, 1)

                    def emit_bcast():
                        nonlocal bc_g
                        bc_g += 1
                        tensor.wait_ge(s_nrm, bc_g)
                        if bc_g > 1:
                            tensor.wait_ge(s_on, bc_g - 1)
                        tensor.matmul(
                            B6[0:64, :], ones164[:], recd[:],
                            start=True, stop=True).then_inc(s_bc, 1)

                    SSb = [B01[:, 0:512], B01[:, 512:1024],
                           B23[:, 0:512], B23[:, 512:1024]]
                    for kc in range(NT):
                        for hh in range(2):
                            sc_g += 1
                            Mv = SSb[(sc_g - 1) % 4]
                            if sc_g > 4:
                                tensor.wait_ge(s_px, sc_g - 4)
                            tensor.matmul(
                                Mv,
                                ck_[hh * 64:hh * 64 + 64, sl,
                                    kc * 128:(kc + 1) * 128],
                                cq_[hh * 64:hh * 64 + 64, sl,
                                    qc * 512:(qc + 1) * 512],
                                start=True, stop=True,
                                tile_position=(hh * 64, 0)).then_inc(s_sc, 1)
                        if kc >= 7:
                            emit_av(kc - 7, sc_g - 14)
                        if aq > 0 and kc == 5:
                            emit_bcast()
                        if aq > 0 and kc == 6:
                            emit_bcast()
                    for kcl in range(NT - 7, NT):
                        emit_av(kcl, sc_g - 2 * (NT - 1 - kcl))
            for _ in range(2):
                bc_g += 1
                tensor.wait_ge(s_nrm, bc_g)
                tensor.wait_ge(s_on, bc_g - 1)
                tensor.matmul(
                    B6[0:64, :], ones164[:], recd[:],
                    start=True, stop=True).then_inc(s_bc, 1)
            tensor.wait_ge(s_wo, 16)
            for t in range(NT):
                sl = t % 2
                s_dl = s_dl0 if sl == 0 else s_dl1
                tensor.wait_ge(s_dl, 16 * (t // 2 + 1))
                if t >= 2:
                    tensor.wait_ge(s_ep, t - 1)
                Pd = B01 if t % 2 == 0 else B23
                for h2 in range(2):
                    Mv = Pd[:, h2 * 512:(h2 + 1) * 512]
                    for cc in range(NCC):
                        inst = tensor.matmul(
                            Mv, olh_[:, sl, cc, :],
                            wob_[:, cc, h2 * 512:(h2 + 1) * 512],
                            start=(cc == 0), stop=(cc == NCC - 1))
                    inst.then_inc(s_dmm, 1)

        @block.scalar
        def _(scalar):
            scalar.wait_ge(s_k, 1)
            actn = [0]

            def fence(inst):
                actn[0] += 1
                inst.then_inc(s_act, 1)
                scalar.wait_ge(s_act, actn[0])

            for q in range(NQ):
                for j in range(JQ):
                    gj = q * JQ + j
                    scalar.wait_ge(s_red, gj + 1)
                    fence(scalar.mul(negm[:, j:j + 1], sums[:, j:j + 1],
                                     -1.0 / D))
                    fence(scalar.activation(sqdump[:], xb[:, j, :],
                                            AF.Square, bias=negm[:, j:j + 1],
                                            accum_out=ssq[:, j:j + 1]))
                    fence(scalar.mul(rstd[:, j:j + 1], ssq[:, j:j + 1],
                                     1.0 / D))
                    fence(scalar.add(rstd[:, j:j + 1], rstd[:, j:j + 1],
                                     1e-5))
                    fence(scalar.activation(rstd[:, j:j + 1],
                                            rstd[:, j:j + 1], AF.Ln))
                    scalar.activation(rstd[:, j:j + 1], rstd[:, j:j + 1],
                                      AF.Exp, scale=-0.5).then_inc(s_stats, 1)
            PSLOT = [B01[:, 0:512], B01[:, 512:1024], B4[:, :], B5[:, :]]
            for p in range(NP):
                for mi in range(3):
                    for qc in range(8):
                        g = (p * 3 + mi) * 8 + qc + 1
                        Mv = PSLOT[(g - 1) % 4]
                        scalar.wait_ge(s_Bmm, g)
                        if p > 0 and mi == 0 and qc == 0:
                            scalar.wait_ge(s_Bst, 64 * p)
                        if L1 and mi < 2:
                            dst = raw_
                        elif mi == 0:
                            dst = qT_
                        elif mi == 1:
                            dst = kT_
                        else:
                            dst = raw_
                        if L1 and mi == 1:
                            scalar.wait_ge(s_rope, p * 16 + qc + 1)
                        if L1 and mi == 2:
                            scalar.wait_ge(s_rope, p * 16 + 8 + qc + 1)
                        scalar.activation(
                            dst[:, qc * 512:(qc + 1) * 512], Mv,
                            AF.Identity,
                            bias=bqkv[:, p, mi:mi + 1]).then_inc(s_Bev, 1)
                    if mi < 2:
                        src = qT_ if mi == 0 else kT_
                        for qc in range(8):
                            n = (p * 2 + mi) * 8 + qc + 1
                            if L1:
                                scalar.wait_ge(s_rope, (p * 2 + mi) * 8
                                               + qc + 1)
                            else:
                                scalar.wait_ge(s_Bev,
                                               (p * 3 + mi) * 8 + qc + 1)
                            if n > 1:
                                scalar.wait_ge(s_nmm, n - 1)
                            scalar.activation(
                                sq[:], src[:, qc * 512:(qc + 1) * 512],
                                AF.Square).then_inc(s_sqn, 1)
                        if mi == 1:
                            scalar.wait_ge(s_nprod, p + 1)
                            fence(scalar.activation(ncl[0:1, 3:4],
                                                    ncl[0:1, 2:3], AF.Ln))
                            fence(scalar.activation(ncl[0:1, 3:4],
                                                    ncl[0:1, 3:4],
                                                    AF.Exp, scale=0.5))
                            scalar.mul(ncl[0:1, 3:4], ncl[0:1, 3:4],
                                       -0.125).then_inc(s_nsq, 1)
            px_g = 0
            cev_g = 0
            for p in range(NP):
                sl = p % 2
                s_cl = s_cl0 if sl == 0 else s_cl1
                scalar.wait_ge(s_cl, 64 * (p // 2 + 1))
                SSb = [B01[:, 0:512], B01[:, 512:1024],
                       B23[:, 0:512], B23[:, 512:1024]]

                def emit_cev(aq_done, hh):
                    nonlocal cev_g
                    cev_g += 1
                    scalar.wait_ge(s_avm, 32 * (aq_done + 1))
                    if cev_g > 1:
                        scalar.wait_ge(s_on, cev_g - 1)
                    AV = B4 if hh == 0 else B5
                    scalar.activation(aev[0:64, :], AV[0:64, :],
                                      AF.Identity).then_inc(s_cev, 1)

                for qc in range(8):
                    aq = p * 8 + qc
                    for kc in range(NT):
                        for hh in range(2):
                            px_g += 1
                            scalar.wait_ge(s_sc, px_g)
                            if px_g > 18:
                                scalar.wait_ge(s_avm, (px_g - 17) // 2)
                            scalar.activation(
                                pbuf_[:, (px_g - 1) % 18, :],
                                SSb[(px_g - 1) % 4], AF.Exp, scale=0.125,
                                bias=cnegc[:, sl:sl + 1]).then_inc(s_px, 1)
                        if aq > 0 and kc == 3:
                            emit_cev(aq - 1, 0)
                        if aq > 0 and kc == 4:
                            emit_cev(aq - 1, 1)
                if p == NP - 1:
                    emit_cev(p * 8 + 7, 0)
                    emit_cev(p * 8 + 7, 1)
            if layer == 0:
                for t in range(NT):
                    c = t % 2
                    scalar.wait_ge(s_red, NT + t + 1)
                    if t > 1:
                        scalar.wait_ge(s_ep, t - 1)
                    fence(scalar.mul(negm[:, c:c + 1], sums[:, c:c + 1],
                                     -1.0 / D))
                    fence(scalar.activation(sqdump[:], outb[:, c, :],
                                            AF.Square, bias=negm[:, c:c + 1],
                                            accum_out=ssq[:, c:c + 1]))
                    fence(scalar.mul(rstd[:, c:c + 1], ssq[:, c:c + 1],
                                     1.0 / D))
                    fence(scalar.add(rstd[:, c:c + 1], rstd[:, c:c + 1],
                                     1e-5))
                    fence(scalar.activation(rstd[:, c:c + 1],
                                            rstd[:, c:c + 1], AF.Ln))
                    scalar.activation(rstd[:, c:c + 1], rstd[:, c:c + 1],
                                      AF.Exp, scale=-0.5).then_inc(s_stats, 1)

        @block.vector
        def _(vector):
            vector.wait_ge(s_k, 2)
            vfn = [0]

            def vfence(inst):
                vfn[0] += 1
                inst.then_inc(s_vf, 1)
                vector.wait_ge(s_vf, vfn[0])

            for q in range(NQ):
                for j in range(JQ):
                    vector.wait_ge(s_x, 32 * (q + 1))
                    vector.reduce_sum(sums[:, j:j + 1], xb[:, j, :],
                                      axis=AX.X).then_inc(s_red, 1)
                for j in range(JQ):
                    gj = q * JQ + j
                    vector.wait_ge(s_stats, gj + 1)
                    if layer == 0:
                        vector.wait_ge(s_pe, 16 * (gj + 1))
                        vector.tensor_scalar(
                            xb[:, j, :], xb[:, j, :], negm[:, j:j + 1],
                            rstd[:, j:j + 1], AluOpType.add, AluOpType.mult)
                        vector.tensor_tensor(
                            xb[:, j, :], xb[:, j, :], peb[:, gj % 2, :],
                            AluOpType.add)
                        vector.tensor_copy(
                            xnbf_[:, j, :], xb[:, j, :]).then_inc(s_xn, 1)
                    else:
                        vector.tensor_scalar(
                            xnbf_[:, j, :], xb[:, j, :], negm[:, j:j + 1],
                            rstd[:, j:j + 1], AluOpType.add,
                            AluOpType.mult).then_inc(s_xn, 1)
                for j in range(JQ):
                    gj = q * JQ + j
                    vector.wait_ge(s_tp, gj + 1)
                    vector.tensor_copy(
                        hT_[:, :, gj * 128:(gj + 1) * 128], B7[:],
                    ).then_inc(s_ev, 1)
            for p in range(NP):
                for mi in range(2):
                    if L1:
                        dst = qT_ if mi == 0 else kT_
                        vector.wait_ge(s_tab, 48)
                        for qc in range(8):
                            rg = (p * 2 + mi) * 8 + qc + 1
                            vector.wait_ge(s_rot, rg)
                            vector.tensor_tensor(
                                rt1[:], raw_[:, qc * 512:(qc + 1) * 512],
                                ropeC[:, qc * 512:(qc + 1) * 512],
                                AluOpType.mult)
                            vector.tensor_tensor(
                                dst[:, qc * 512:(qc + 1) * 512], B6[:],
                                ropeS[:, qc * 512:(qc + 1) * 512],
                                AluOpType.mult)
                            vector.tensor_tensor(
                                dst[:, qc * 512:(qc + 1) * 512],
                                dst[:, qc * 512:(qc + 1) * 512], rt1[:],
                                AluOpType.add).then_inc(s_rope, 1)
                    for qc in range(8):
                        n = (p * 2 + mi) * 8 + qc + 1
                        vector.wait_ge(s_nmm, n)
                        vector.reduce_max(nmx[0:1, qc:qc + 1],
                                          B23[0:1, 0:512],
                                          axis=AX.X).then_inc(s_nred, 1)
                    vector.wait_ge(s_nred, (p * 2 + mi) * 8 + 8)
                    vfence(vector.reduce_max(ncl[0:1, mi:mi + 1],
                                             nmx[0:1, 0:8], axis=AX.X))
                vector.tensor_tensor(ncl[0:1, 2:3], ncl[0:1, 0:1],
                                     ncl[0:1, 1:2],
                                     AluOpType.mult).then_inc(s_nprod, 1)
                vector.wait_ge(s_ngmm, p + 1)
                if p > 0:
                    vector.wait_ge(s_Bst, 64 * p)
                vector.tensor_copy(negc[:], B6[:, 0:1]).then_inc(s_ngc, 1)
                for tg in range(4):
                    vector.wait_ge(s_vtp, 4 * p + tg + 1)
                    if p > 0 and tg == 0:
                        vector.wait_ge(s_Bst, 64 * p)
                    vector.tensor_copy(
                        vag_[:, tg * 8:(tg + 1) * 8, 0:64], B7[:, :, 0:64])
                    vector.tensor_copy(
                        vag_[:, tg * 8:(tg + 1) * 8, 65:129],
                        B7[:, :, 64:128]).then_inc(s_vev, 1)
            on_g = 0
            for p in range(NP):
                for qc in range(8):
                    aq = p * 8 + qc
                    for hh in range(2):
                        AV = B4 if hh == 0 else B5
                        vector.wait_ge(s_cev, 2 * aq + hh + 1)
                        if 2 * aq + hh > 0:
                            vector.wait_ge(s_bc, 2 * aq + hh)
                        with nc.allow_low_precision(
                                reason="bf16 softmax denom for bcast matmul"):
                            vector.reciprocal(recd[:],
                                              AV[64:65, :]).then_inc(s_nrm, 1)
                        on_g += 1
                        vector.wait_ge(s_bc, on_g)
                        if on_g == 1 and p == 0:
                            pass
                        if qc == 0 and hh == 0 and p > 0:
                            vector.wait_ge(s_ost, 16 * p)
                        vector.tensor_tensor(
                            onT_.rearrange("p (h q) -> p h q", h=8)[
                                hh * 64:(hh + 1) * 64, qc // 1, :]
                            if False else
                            onT_[hh * 64:(hh + 1) * 64,
                                 qc * 512:(qc + 1) * 512],
                            aev[0:64, :], B6[0:64, :],
                            AluOpType.mult).then_inc(s_on, 1)
            for t in range(NT):
                sl = t % 2
                s_hr = s_hr0 if sl == 0 else s_hr1
                vector.wait_ge(s_dmm, 2 * (t + 1))
                vector.wait_ge(s_hr, 16 * (t // 2 + 1))
                if t >= 2:
                    vector.wait_ge(s_st0 if t % 2 == 0 else s_st1,
                                   16 * (t // 2))
                Pd = B01 if t % 2 == 0 else B23
                c = t % 2
                yb = outb[:, c, :]
                if layer == 0:
                    s_xr = s_xr0 if sl == 0 else s_xr1
                    vector.wait_ge(s_xr, 16 * (t // 2 + 1))
                    vector.tensor_tensor(yb, hb[:, sl, :], Pd[:],
                                         AluOpType.add)
                    vector.tensor_tensor(yb, yb, bo[:], AluOpType.add)
                    vector.reduce_sum(sums[:, c:c + 1], yb,
                                      axis=AX.X).then_inc(s_red, 1)
                    vector.wait_ge(s_stats, NT + t + 1)
                    vector.tensor_scalar(yb, yb, negm[:, c:c + 1],
                                         rstd[:, c:c + 1], AluOpType.add,
                                         AluOpType.mult)
                    vector.tensor_tensor(yb, yb, gb[:, 0, :],
                                         AluOpType.mult)
                    vector.tensor_tensor(yb, yb, gb[:, 1, :], AluOpType.add)
                    vector.tensor_tensor(yb, xr2[:, sl, :], yb,
                                         AluOpType.add).then_inc(s_ep, 1)
                else:
                    vector.tensor_tensor(yb, hb[:, sl, :], Pd[:],
                                         AluOpType.add)
                    vector.tensor_tensor(yb, yb, bo[:],
                                         AluOpType.add).then_inc(s_ep, 1)

    ph.close()
    return ph



def _attn_host_inputs(inputs, pfx, consts):
    bqkv = np.zeros((128, 8, 3), np.float32)
    for mi, nm in enumerate(["bq", "bk", "bv"]):
        bqkv[:, :, mi] = inputs[pfx + nm].reshape(8, 128).T
    bo_rep = np.ascontiguousarray(np.broadcast_to(
        inputs[pfx + "bo"][None, :].astype(np.float32), (128, D)))
    base = {
        "wq": inputs[pfx + "Wq"], "wk": inputs[pfx + "Wk"],
        "wv": inputs[pfx + "Wv"], "wo": inputs[pfx + "Wo"],
        "bqkv": bqkv, "bo": bo_rep, "ident": consts["ident"],
        "o164": np.ones((1, 64), dtype=BF),
        "o128": np.ones((128, 1), dtype=BF),
        "o1128": np.ones((1, 128), np.float32),
    }
    return base


def _declare_attn_tensors(nc, layer):
    t = {}
    t["x"] = nc.dram_tensor("x", [S, D], F32, kind="ExternalInput").ap()
    for nm in ["wq", "wk", "wv", "wo"]:
        t[nm] = nc.dram_tensor(nm, [D, D], F32, kind="ExternalInput").ap()
    t["bqkv"] = nc.dram_tensor("bqkv", [128, 8, 3], F32,
                               kind="ExternalInput").ap()
    t["bo"] = nc.dram_tensor("bo", [128, D], F32, kind="ExternalInput").ap()
    t["ident"] = nc.dram_tensor("ident", [128, 128], BF16,
                                kind="ExternalInput").ap()
    t["o164"] = nc.dram_tensor("o164", [1, 64], BF16,
                               kind="ExternalInput").ap()
    t["o128"] = nc.dram_tensor("o128", [128, 1], BF16,
                               kind="ExternalInput").ap()
    t["o1128"] = nc.dram_tensor("o1128", [1, 128], F32,
                                kind="ExternalInput").ap()
    t["scr_q"] = nc.dram_tensor("scr_q", [128, 8, S], BF16,
                                kind="Internal").ap()
    t["scr_k"] = nc.dram_tensor("scr_k", [128, 8, S], BF16,
                                kind="Internal").ap()
    t["scr_v"] = nc.dram_tensor("scr_v", [128, 8, 32, 130], BF16,
                                kind="Internal").ap()
    t["scr_o"] = nc.dram_tensor("scr_o", [128, 8, S], BF16,
                                kind="Internal").ap()
    t["scr_c"] = nc.dram_tensor("scr_c", [128, 8], F32,
                                kind="Internal").ap()
    t["xout"] = nc.dram_tensor("xout", [S, D], F32,
                               kind="ExternalOutput").ap()
    if layer == 0:
        t["pe"] = nc.dram_tensor("pe", [S, D], F32,
                                 kind="ExternalInput").ap()
        t["hscr"] = nc.dram_tensor("hscr", [S, D], F32, kind="Internal").ap()
        t["gb"] = nc.dram_tensor("gb", [128, 2, D], F32,
                                 kind="ExternalInput").ap()
    else:
        t["ropeC"] = nc.dram_tensor("ropeC", [128, S], F32,
                                    kind="ExternalInput").ap()
        t["ropeS"] = nc.dram_tensor("ropeS", [128, S], F32,
                                    kind="ExternalInput").ap()
        t["swapP"] = nc.dram_tensor("swapP", [128, 128], BF16,
                                    kind="ExternalInput").ap()
    return t


def kernel(**inputs):
    if FUSED:
        return _kernel_fused(**inputs)
    return _kernel_unfused(**inputs)


FUSED = True


def _kernel_fused(**inputs):
    consts = host_constants()
    nc = bass.Bass("TRN2", target_bir_lowering=False)

    EI, EO, IN = "ExternalInput", "ExternalOutput", "Internal"
    t = {}
    t["x"] = nc.dram_tensor("x", [S, D], F32, kind=EI).ap()
    for L, pfx in [(0, "a0"), (1, "a1")]:
        for nm in ["wq", "wk", "wv", "wo"]:
            t[pfx + nm] = nc.dram_tensor(pfx + nm, [D, D], F32, kind=EI).ap()
        t[pfx + "bqkv"] = nc.dram_tensor(pfx + "bqkv", [128, 8, 3], F32,
                                         kind=EI).ap()
        t[pfx + "bo"] = nc.dram_tensor(pfx + "bo", [128, D], F32,
                                       kind=EI).ap()
    t["ident"] = nc.dram_tensor("ident", [128, 128], BF16, kind=EI).ap()
    t["o164"] = nc.dram_tensor("o164", [1, 64], BF16, kind=EI).ap()
    t["o128"] = nc.dram_tensor("o128", [128, 1], BF16, kind=EI).ap()
    t["o1128"] = nc.dram_tensor("o1128", [1, 128], F32, kind=EI).ap()
    t["pe"] = nc.dram_tensor("pe", [S, D], F32, kind=EI).ap()
    t["gb"] = nc.dram_tensor("gb", [128, 2, D], F32, kind=EI).ap()
    t["ropeC"] = nc.dram_tensor("ropeC", [128, S], F32, kind=EI).ap()
    t["ropeS"] = nc.dram_tensor("ropeS", [128, S], F32, kind=EI).ap()
    t["swapP"] = nc.dram_tensor("swapP", [128, 128], BF16, kind=EI).ap()
    t["gateT"] = nc.dram_tensor("gateT", [D, 8], F32, kind=EI).ap()
    t["wg"] = nc.dram_tensor("wg", [8, D, 1408], F32, kind=EI).ap()
    t["wu"] = nc.dram_tensor("wu", [8, D, 1408], F32, kind=EI).ap()
    t["wd"] = nc.dram_tensor("wd", [8, 1408, D], F32, kind=EI).ap()
    t["swg"] = nc.dram_tensor("swg", [D, 2816], F32, kind=EI).ap()
    t["swu"] = nc.dram_tensor("swu", [D, 2816], F32, kind=EI).ap()
    t["swd"] = nc.dram_tensor("swd", [2816, D], F32, kind=EI).ap()
    t["wrT"] = nc.dram_tensor("wrT", [D, D], F32, kind=EI).ap()
    t["wkT"] = nc.dram_tensor("wkT", [D, D], F32, kind=EI).ap()
    t["wvT"] = nc.dram_tensor("wvT", [D, D], F32, kind=EI).ap()
    t["mur"] = nc.dram_tensor("mur", [D], F32, kind=EI).ap()
    t["muk"] = nc.dram_tensor("muk", [D], F32, kind=EI).ap()
    x1_d = nc.dram_tensor("x1s", [S, D], F32, kind=IN).ap()
    x2_d = nc.dram_tensor("x2s", [S, D], F32, kind=IN).ap()
    x3_d = nc.dram_tensor("x3s", [S, D], F32, kind=IN).ap()
    h_d = nc.dram_tensor("hscr", [S, D], F32, kind=IN).ap()
    scr = {}
    for nm, shp, dt in [("q", [128, 8, S], BF16), ("k", [128, 8, S], BF16),
                        ("v", [128, 8, 32, 130], BF16),
                        ("o", [128, 8, S], BF16), ("c", [128, 8], F32)]:
        scr[nm] = nc.dram_tensor("scr_" + nm, shp, dt, kind=IN).ap()
    out_d = nc.dram_tensor("out", [S, D], F32, kind=EO).ap()

    def free_phase_sems(ph):
        handles = list(ph.sems.values())
        with nc.Block() as blk:
            @blk.gpsimd
            def _(g):
                nc.clear_and_free_semaphores(handles)

    sem_leak = contextlib.ExitStack()
    with nc.sbuf_tensor("gcbuf", [128, 4], F32) as gcbuf:
        ph = build_attn(nc, 0, t["x"], x1_d, t["a0wq"], t["a0wk"],
                        t["a0wv"], t["a0wo"], t["a0bqkv"], t["a0bo"],
                        t["ident"], t["o164"], t["o128"], t["o1128"],
                        scr["q"], scr["k"], scr["v"], scr["o"], scr["c"],
                        pe_ap=t["pe"], h_ap=h_d, gb_ap=t["gb"], tag="pa0",
                        sem_stack=sem_leak, gcbuf=gcbuf)
        free_phase_sems(ph)
        ph = build_phase1(nc, x1_d, x2_d, t["gateT"], t["wg"], t["wu"],
                          t["wd"], t["swg"], t["swu"], t["swd"], t["ident"],
                          tag="p1", sem_stack=sem_leak, gcbuf=gcbuf)
        free_phase_sems(ph)
        ph = build_attn(nc, 1, x2_d, x3_d, t["a1wq"], t["a1wk"], t["a1wv"],
                        t["a1wo"], t["a1bqkv"], t["a1bo"], t["ident"],
                        t["o164"], t["o128"], t["o1128"], scr["q"], scr["k"],
                        scr["v"], scr["o"], scr["c"], ropeC_ap=t["ropeC"],
                        ropeS_ap=t["ropeS"], swap_ap=t["swapP"], tag="pa1",
                        sem_stack=sem_leak, gcbuf=gcbuf)
        free_phase_sems(ph)
        build_phase3(nc, x3_d, out_d, t["wrT"], t["wkT"], t["wvT"],
                     t["mur"], t["muk"], t["ident"], tag="p3",
                     sem_stack=sem_leak, gcbuf=gcbuf)

    base = {
        "ident": consts["ident"], "o164": np.ones((1, 64), dtype=BF),
        "o128": np.ones((128, 1), dtype=BF),
        "o1128": np.ones((1, 128), np.float32),
        "pe": consts["pe"], "ropeC": consts["ropeC"],
        "ropeS": consts["ropeS"], "swapP": consts["swapP"],
        "gateT": np.ascontiguousarray(inputs["L0_ffn_gate_w"].T),
        "wg": inputs["L0_ffn_wg"], "wu": inputs["L0_ffn_wu"],
        "wd": inputs["L0_ffn_wd"], "swg": inputs["L0_ffn_swg"],
        "swu": inputs["L0_ffn_swu"], "swd": inputs["L0_ffn_swd"],
        "wrT": np.ascontiguousarray(inputs["L1_ffn_W_r"].T),
        "wkT": np.ascontiguousarray(inputs["L1_ffn_W_k"].T),
        "wvT": np.ascontiguousarray(inputs["L1_ffn_W_v"].T),
        "mur": inputs["L1_ffn_mu_r"], "muk": inputs["L1_ffn_mu_k"],
    }
    for L, pfx, ipfx in [(0, "a0", "L0_attn_"), (1, "a1", "L1_attn_")]:
        bqkv = np.zeros((128, 8, 3), np.float32)
        for mi, nm in enumerate(["bq", "bk", "bv"]):
            bqkv[:, :, mi] = inputs[ipfx + nm].reshape(8, 128).T
        base[pfx + "bqkv"] = bqkv
        base[pfx + "bo"] = np.ascontiguousarray(np.broadcast_to(
            inputs[ipfx + "bo"][None, :].astype(np.float32), (128, D)))
        for nm in ["wq", "wk", "wv", "wo"]:
            base[pfx + nm] = inputs[ipfx + nm.capitalize()[0] + nm[1]]
    gbarr = np.stack([
        np.broadcast_to(inputs["L0_attn_gamma"][None, :], (128, D)),
        np.broadcast_to(inputs["L0_attn_beta"][None, :], (128, D))],
        axis=1).astype(np.float32)
    base["gb"] = np.ascontiguousarray(gbarr)

    x = inputs["L0_attn_x"]
    in_maps = [dict(base, x=np.ascontiguousarray(x[c])) for c in range(NC)]
    res = run_bass_kernel_spmd(nc, in_maps, core_ids=list(range(NC)))
    out = np.stack([res.results[c]["out"] for c in range(NC)])
    return out.astype(np.float32)


def _kernel_unfused(**inputs):
    consts = host_constants()

    nc0 = bass.Bass("TRN2", target_bir_lowering=False)
    t0 = _declare_attn_tensors(nc0, 0)
    build_attn(nc0, 0, t0["x"], t0["xout"], t0["wq"], t0["wk"], t0["wv"],
               t0["wo"], t0["bqkv"], t0["bo"], t0["ident"], t0["o164"],
               t0["o128"], t0["o1128"], t0["scr_q"], t0["scr_k"], t0["scr_v"],
               t0["scr_o"], t0["scr_c"], pe_ap=t0["pe"], h_ap=t0["hscr"],
               gb_ap=t0["gb"], tag="pa")
    base0 = _attn_host_inputs(inputs, "L0_attn_", consts)
    base0["pe"] = consts["pe"]
    gb = np.stack([
        np.broadcast_to(inputs["L0_attn_gamma"][None, :], (128, D)),
        np.broadcast_to(inputs["L0_attn_beta"][None, :], (128, D))],
        axis=1).astype(np.float32)
    base0["gb"] = np.ascontiguousarray(gb)
    x = inputs["L0_attn_x"]
    in_maps = [dict(base0, x=np.ascontiguousarray(x[c])) for c in range(NC)]
    res = run_bass_kernel_spmd(nc0, in_maps, core_ids=list(range(NC)))
    x1 = [res.results[c]["xout"] for c in range(NC)]

    nc1 = bass.Bass("TRN2", target_bir_lowering=False)
    x1_t = nc1.dram_tensor("x1", [S, D], F32, kind="ExternalInput").ap()
    gateT_t = nc1.dram_tensor("gateT", [D, 8], F32,
                              kind="ExternalInput").ap()
    wg_t = nc1.dram_tensor("wg", [8, D, 1408], F32,
                           kind="ExternalInput").ap()
    wu_t = nc1.dram_tensor("wu", [8, D, 1408], F32,
                           kind="ExternalInput").ap()
    wd_t = nc1.dram_tensor("wd", [8, 1408, D], F32,
                           kind="ExternalInput").ap()
    swg_t = nc1.dram_tensor("swg", [D, 2816], F32, kind="ExternalInput").ap()
    swu_t = nc1.dram_tensor("swu", [D, 2816], F32, kind="ExternalInput").ap()
    swd_t = nc1.dram_tensor("swd", [2816, D], F32, kind="ExternalInput").ap()
    ident1_t = nc1.dram_tensor("ident", [128, 128], BF16,
                               kind="ExternalInput").ap()
    x2_t = nc1.dram_tensor("x2", [S, D], F32, kind="ExternalOutput").ap()
    build_phase1(nc1, x1_t, x2_t, gateT_t, wg_t, wu_t, wd_t, swg_t, swu_t,
                 swd_t, ident1_t)
    base1 = {
        "gateT": np.ascontiguousarray(inputs["L0_ffn_gate_w"].T),
        "wg": inputs["L0_ffn_wg"], "wu": inputs["L0_ffn_wu"],
        "wd": inputs["L0_ffn_wd"], "swg": inputs["L0_ffn_swg"],
        "swu": inputs["L0_ffn_swu"], "swd": inputs["L0_ffn_swd"],
        "ident": consts["ident"],
    }
    in_maps = [dict(base1, x1=x1[c]) for c in range(NC)]
    res = run_bass_kernel_spmd(nc1, in_maps, core_ids=list(range(NC)))
    x2 = [res.results[c]["x2"] for c in range(NC)]

    nc2 = bass.Bass("TRN2", target_bir_lowering=False)
    t2 = _declare_attn_tensors(nc2, 1)
    build_attn(nc2, 1, t2["x"], t2["xout"], t2["wq"], t2["wk"], t2["wv"],
               t2["wo"], t2["bqkv"], t2["bo"], t2["ident"], t2["o164"],
               t2["o128"], t2["o1128"], t2["scr_q"], t2["scr_k"], t2["scr_v"],
               t2["scr_o"], t2["scr_c"], ropeC_ap=t2["ropeC"],
               ropeS_ap=t2["ropeS"], swap_ap=t2["swapP"], tag="pa")
    base2 = _attn_host_inputs(inputs, "L1_attn_", consts)
    base2["ropeC"] = consts["ropeC"]
    base2["ropeS"] = consts["ropeS"]
    base2["swapP"] = consts["swapP"]
    in_maps = [dict(base2, x=x2[c]) for c in range(NC)]
    res = run_bass_kernel_spmd(nc2, in_maps, core_ids=list(range(NC)))
    x3 = [res.results[c]["xout"] for c in range(NC)]

    nc3 = bass.Bass("TRN2", target_bir_lowering=False)
    x3_t = nc3.dram_tensor("x3", [S, D], F32, kind="ExternalInput").ap()
    wrT_t = nc3.dram_tensor("wrT", [D, D], F32, kind="ExternalInput").ap()
    wkT_t = nc3.dram_tensor("wkT", [D, D], F32, kind="ExternalInput").ap()
    wvT_t = nc3.dram_tensor("wvT", [D, D], F32, kind="ExternalInput").ap()
    mur_t = nc3.dram_tensor("mur", [D], F32, kind="ExternalInput").ap()
    muk_t = nc3.dram_tensor("muk", [D], F32, kind="ExternalInput").ap()
    ident3_t = nc3.dram_tensor("ident", [128, 128], BF16,
                               kind="ExternalInput").ap()
    out_t = nc3.dram_tensor("out", [S, D], F32, kind="ExternalOutput").ap()
    build_phase3(nc3, x3_t, out_t, wrT_t, wkT_t, wvT_t, mur_t, muk_t,
                 ident3_t)
    base3 = {
        "wrT": np.ascontiguousarray(inputs["L1_ffn_W_r"].T),
        "wkT": np.ascontiguousarray(inputs["L1_ffn_W_k"].T),
        "wvT": np.ascontiguousarray(inputs["L1_ffn_W_v"].T),
        "mur": inputs["L1_ffn_mu_r"], "muk": inputs["L1_ffn_mu_k"],
        "ident": consts["ident"],
    }
    in_maps = [dict(base3, x3=x3[c]) for c in range(NC)]
    res = run_bass_kernel_spmd(nc3, in_maps, core_ids=list(range(NC)))
    out = np.stack([res.results[c]["out"] for c in range(NC)])
    return out.astype(np.float32)
